# version 06
# baseline (speedup 1.0000x reference)
import numpy as np
import ml_dtypes
import concourse.bass as bass
import concourse.mybir as mybir
from concourse.bass_utils import run_bass_kernel_spmd

F32 = mybir.dt.float32
BF16 = mybir.dt.bfloat16
I32 = mybir.dt.int32
AF = mybir.ActivationFunctionType
ALU = mybir.AluOpType
AX = mybir.AxisListType.X
BF = ml_dtypes.bfloat16
f32 = np.float32

NT = 8
SCALE = f32(1.0 / np.sqrt(128.0))
EBIAS = f32(-15.0)
NEG = -1.0e30
AGZ = 1024 * 1024


def build_nc():
    nc = bass.Bass("TRN2", target_bir_lowering=False, debug=False)

    xsh = nc.dram_tensor("xsh", [1024, 4096], F32, kind="ExternalInput").ap()
    qkvw = nc.dram_tensor("qkvw", [2, 12, 128, 32, 512], F32, kind="ExternalInput").ap()
    wow = nc.dram_tensor("wow", [2, 8, 128, 32, 512], F32, kind="ExternalInput").ap()
    ffnw = nc.dram_tensor("ffnw", [2, 8, 128, 32, 512], F32, kind="ExternalInput").ap()
    acat = nc.dram_tensor("acat", [2, 128, 32, 128], F32, kind="ExternalInput").ap()
    bcat = nc.dram_tensor("bcat", [2, 128, 4096], F32, kind="ExternalInput").ap()
    rwt = nc.dram_tensor("rwt", [128, 32, 15], F32, kind="ExternalInput").ap()
    biasv = nc.dram_tensor("biasv", [2, 1, 4096], F32, kind="ExternalInput").ap()
    rbias = nc.dram_tensor("rbias", [1, 15], F32, kind="ExternalInput").ap()
    pcolv = nc.dram_tensor("pcolv", [128, 1], F32, kind="ExternalInput").ap()
    emap = nc.dram_tensor("emap", [15, 1], I32, kind="ExternalInput").ap()
    identc = nc.dram_tensor("identc", [128, 128], BF16, kind="ExternalInput").ap()
    atric = nc.dram_tensor("atric", [128, 128], BF16, kind="ExternalInput").ap()
    onesc = nc.dram_tensor("onesc", [1, 128], BF16, kind="ExternalInput").ap()
    esbc = nc.dram_tensor("esbc", [8, 128], BF16, kind="ExternalInput").ap()
    cosc = nc.dram_tensor("cosc", [128, 512], F32, kind="ExternalInput").ap()
    sinc = nc.dram_tensor("sinc", [128, 512], F32, kind="ExternalInput").ap()
    mmaskc = nc.dram_tensor("mmaskc", [128, 512], BF16, kind="ExternalInput").ap()
    cimsc = nc.dram_tensor("cimsc", [128, 13], F32, kind="ExternalInput").ap()
    y = nc.dram_tensor("y", [1024, 4096], F32, kind="ExternalOutput").ap()

    qtp = nc.dram_tensor("qtp", [8, 32, 128, 128], BF16, kind="Internal").ap()
    ag_in_t = nc.dram_tensor("ag_in", [2 * AGZ], BF16, kind="Internal")
    ag_out_t = nc.dram_tensor("ag_out", [8 * AGZ], BF16, kind="Internal")
    zpark = nc.dram_tensor("zpark", [8, 128, 4096], BF16, kind="Internal").ap()

    ag_in = ag_in_t.ap()
    agi_kt = ag_in[0:AGZ].rearrange("(g d t k) -> g d t k", g=8, d=128, t=8, k=128)
    agi_v = ag_in[AGZ:2 * AGZ].rearrange("(t k v) -> t k v", t=8, k=128, v=1024)
    ago5 = ag_out_t.ap().rearrange("(r s z) -> r s z", r=4, s=2, z=AGZ)
    ago_kt = ago5[:, 0, :].rearrange("r (g d t k) -> r g d t k", g=8, d=128, t=8, k=128)
    ago_v = ago5[:, 1, :].rearrange("r (t k v) -> r t k v", t=8, k=128, v=1024)

    A = nc.alloc_sbuf_tensor
    Xsb = A("Xsb", [128, 8 * 4096], BF16)
    cos_sb = A("cos_sb", [128, 512], F32)
    sin_sb = A("sin_sb", [128, 512], F32)
    id_sb = A("id_sb", [128, 128], BF16)
    atri_sb = A("atri_sb", [128, 128], BF16)
    mmask_sb = A("mmask_sb", [128, 512], BF16)
    ones_sb = A("ones_sb", [1, 128], BF16)
    esb = A("esb", [8, 128], BF16)
    pcol_sb = A("pcol_sb", [128, 1], F32)
    bias_sb = A("bias_sb", [1, 512], BF16)
    rb_sb = A("rb_sb", [1, 15], BF16)
    emap_i = A("emap_i", [15, 1], I32)
    emap_f = A("emap_f", [15, 1], F32)
    msb = A("msb", [15, 8], BF16)
    cims = A("cims", [128, 13], F32)
    wb = [A("wb0", [128, 16 * 512], BF16), A("wb1", [128, 16 * 512], BF16)]
    ut_sb = A("ut_sb", [128, 8 * 128], BF16)
    colz = A("colz", [128, 16], F32)
    szc = A("szc", [128, 64], F32)
    smc = A("smc", [128, 64], F32)
    gam = A("gam", [128, 8], F32)
    zc = A("zc", [128, 32], F32)
    rinvq = A("rinvq", [128, 4], F32)
    pr_sb = A("pr_sb", [128, 16], F32)
    psel = A("psel", [128, 16], BF16)
    junk15 = A("junk15", [128, 16], F32)
    rnk = A("rnk", [128, 32], F32)
    rcol = A("rcol", [128, 8], F32)
    pselT = A("pselT", [15, 128], BF16)
    wt_bf = A("wt_bf", [8, 128], BF16)
    junkb = A("junkb", [128, 512], BF16)
    t1f = A("t1f", [128, 128], F32)
    t2f = A("t2f", [128, 128], F32)
    robuf = [A("rob0", [128, 512], BF16), A("rob1", [128, 512], BF16)]
    qstage = [A("qst0", [128, 512], BF16), A("qst1", [128, 512], BF16)]
    kstage = [A("kst0", [128, 512], BF16), A("kst1", [128, 512], BF16)]
    vstage = [A("vst0", [128, 512], BF16), A("vst1", [128, 512], BF16)]
    zstage = vstage
    zld = kstage
    ostage = None
    qtq = [A("qtq0", [128, 512], BF16), A("qtq1", [128, 512], BF16)]
    ostage = robuf[0]
    htmp = robuf[1][:, 0:128]
    ustage = robuf[1][:, 128:256]
    wf_bf = robuf[1][:, 256:384]
    rwt_sb = A("rwt_sb", [128, 480], BF16)

    ARENA_COLS = 45056
    arena = A("arena", [128, ARENA_COLS], BF16)

    def av(off, cols):
        return arena[:, off:off + cols]

    xlnT = av(0, 32768)
    ot4 = av(0, 16384)
    ktg = [av(16384, 4096), av(20480, 4096)]
    vg = [av(24576, 4096), av(28672, 4096)]
    Pst = [av(32768, 4096), av(36864, 4096)]
    pt_sb = av(40960, 4096)
    acat_sb = av(32768, 4096)
    bcat_sb = av(36864, 4096)
    xln_bf = av(40960, 4096)

    psA = nc.alloc_psum_tensor("psA", [128, 512], F32)
    psB = nc.alloc_psum_tensor("psB", [128, 512], F32)
    psT = nc.alloc_psum_tensor("psT", [128, 1024], BF16)
    psU = nc.alloc_psum_tensor("psU", [128, 1024], BF16)
    psS = [nc.alloc_psum_tensor("psS0", [128, 512], F32),
           nc.alloc_psum_tensor("psS1", [128, 512], F32)]
    psO = nc.alloc_psum_tensor("psO", [128, 512], F32)
    psX = nc.alloc_psum_tensor("psX", [128, 512], F32)

    sem_names = ["PE", "VE", "SE", "CC",
                 "sxl", "scl", "scl2", "sw0", "sw1", "skv0", "skv1",
                 "sqt0", "sqt1", "sqs0", "sqs1", "sks0", "sks1",
                 "svs0", "svs1", "szs0", "szs1", "szl0", "szl1", "sys"]
    SM = {n: nc.alloc_semaphore(n) for n in sem_names}
    CNT = {n: 0 for n in sem_names}
    prog = {k: [] for k in "tvsgy"}

    def E(eng, fn):
        prog[eng].append(fn)

    def inc(n, k=1):
        CNT[n] += k
        return CNT[n]

    def cv(n):
        return CNT[n]

    def W(eng, sem, val):
        if val <= 0:
            return
        E(eng, lambda e, s=SM[sem], v=val: e.wait_ge(s, v))

    def mm(out, lhsT, rhs, start, stop, tr=False, pe=False):
        if pe:
            if tr:
                E("t", lambda e, o=out, l=lhsT, r=rhs, a=start, b=stop:
                  e.matmul(o, l, r, start=a, stop=b, is_transpose=True)
                  .then_inc(SM["PE"], 1))
            else:
                E("t", lambda e, o=out, l=lhsT, r=rhs, a=start, b=stop:
                  e.matmul(o, l, r, start=a, stop=b).then_inc(SM["PE"], 1))
            return inc("PE")
        if tr:
            E("t", lambda e, o=out, l=lhsT, r=rhs, a=start, b=stop:
              e.matmul(o, l, r, start=a, stop=b, is_transpose=True))
        else:
            E("t", lambda e, o=out, l=lhsT, r=rhs, a=start, b=stop:
              e.matmul(o, l, r, start=a, stop=b))
        return None

    bank_free = {"psA": ("VE", 0), "psB": ("VE", 0), "psT": ("VE", 0),
                 "psU": ("VE", 0), "psS0": ("SE", 0), "psS1": ("SE", 0),
                 "psO": ("VE", 0), "psX": ("VE", 0)}

    def twb(name):
        eng, c = bank_free[name]
        W("t", eng, c)

    def phase_barrier():
        pe0, ve0, se0 = cv("PE"), cv("VE"), cv("SE")
        W("t", "VE", ve0); W("t", "SE", se0)
        W("v", "PE", pe0); W("v", "SE", se0)
        W("s", "PE", pe0); W("s", "VE", ve0)
        W("y", "PE", pe0); W("y", "VE", ve0); W("y", "SE", se0)
        W("g", "PE", pe0); W("g", "VE", ve0); W("g", "SE", se0)

    wb_free_pe = [0, 0]

    def load_wstrip(src_ap, buf):
        W("g", "PE", wb_free_pe[buf])
        E("g", lambda e, a=src_ap, b=wb[buf], s=SM["sw%d" % buf]:
          e.dma_start(out=b[:, :], in_=a).then_inc(s, 16))
        inc("sw%d" % buf, 16)

    bias_free_pe = [0]

    def load_bias_slice(L, s):
        W("g", "PE", bias_free_pe[0])
        E("g", lambda e, L=L, s=s: e.dma_start(
            out=bias_sb[:, :], in_=biasv[L, :, s * 512:(s + 1) * 512])
          .then_inc(SM["scl2"], 16))
        inc("scl2", 16)

    for t in range(NT):
        E("g", lambda e, t=t: e.dma_start(
            out=Xsb[:, t * 4096:(t + 1) * 4096],
            in_=xsh[t * 128:(t + 1) * 128, :]).then_inc(SM["sxl"], 16))
        inc("sxl", 16)
    for dst, src in [(cos_sb, cosc), (sin_sb, sinc), (id_sb, identc),
                     (atri_sb, atric), (mmask_sb, mmaskc), (ones_sb, onesc),
                     (esb, esbc), (rb_sb, rbias), (pcol_sb, pcolv),
                     (emap_i, emap), (cims, cimsc)]:
        E("g", lambda e, d=dst, s=src: e.dma_start(out=d[:, :], in_=s)
          .then_inc(SM["scl"], 16))
        inc("scl", 16)
    W("s", "scl", cv("scl"))
    E("s", lambda e: e.copy(emap_f[:, :], emap_i[:, :]).then_inc(SM["SE"], 1))
    inc("SE")
    W("v", "SE", cv("SE"))
    for ei in range(8):
        E("v", lambda e, ei=ei: e.tensor_scalar(
            msb[:, ei:ei + 1], emap_f[:, :], cims[0:15, 5 + ei:6 + ei], None,
            ALU.is_equal).then_inc(SM["VE"], 1))
        inc("VE")

    x_ready_ve = [0] * NT
    xlnT_ve = [0] * NT
    xln_free_pe = [0]

    def emit_layernorm(t, first):
        xt = Xsb[:, t * 4096:(t + 1) * 4096]
        su, mu, ms_, vr, sd, rs, sq = (colz[:, 0:1], colz[:, 1:2], colz[:, 2:3],
                                       colz[:, 3:4], colz[:, 4:5], colz[:, 5:6],
                                       colz[:, 6:7])
        if first:
            W("v", "sxl", 16 * NT)
            W("s", "sxl", 16 * NT)
        else:
            W("v", "VE", x_ready_ve[t])
            W("s", "VE", x_ready_ve[t])
        E("v", lambda e, xt=xt: e.reduce_sum(su, xt, axis=AX)
          .then_inc(SM["VE"], 1))
        inc("VE")
        for sl in range(8):
            E("s", lambda e, xt=xt, sl=sl: e.activation(
                junkb[:, :], xt[:, sl * 512:(sl + 1) * 512], AF.Square,
                accum_out=colz[:, 8 + sl:9 + sl])
              .then_inc(SM["SE"], 1) if sl == 7 else
              e.activation(junkb[:, :], xt[:, sl * 512:(sl + 1) * 512],
                           AF.Square, accum_out=colz[:, 8 + sl:9 + sl]))
        inc("SE")
        W("v", "SE", cv("SE"))
        E("v", lambda e: e.reduce_sum(sq, colz[:, 8:16], axis=AX)
          .then_inc(SM["VE"], 1))
        W("v", "VE", inc("VE"))
        E("v", lambda e: e.tensor_scalar(mu, su, cims[:, 0:1], None, ALU.mult))
        E("v", lambda e: e.tensor_scalar(ms_, sq, cims[:, 0:1], None, ALU.mult))
        E("v", lambda e: e.scalar_tensor_tensor(vr, mu, mu[:, 0:1], ms_,
                                                ALU.mult, ALU.subtract)
          .then_inc(SM["VE"], 1))
        inc("VE")
        W("s", "VE", cv("VE"))
        E("s", lambda e: e.activation(sd, vr, AF.Sqrt, bias=f32(1e-5),
                                      scale=f32(-1.0)).then_inc(SM["SE"], 1))
        inc("SE")
        W("v", "SE", cv("SE"))
        W("v", "PE", xln_free_pe[0])
        E("v", lambda e: e.reciprocal(rs, sd))
        E("v", lambda e, xt=xt: e.tensor_scalar(xln_bf[:, :], xt, mu[:, 0:1],
                                                rs[:, 0:1], ALU.subtract,
                                                ALU.mult).then_inc(SM["VE"], 1))
        ve_ln = inc("VE")
        W("t", "VE", ve_ln)
        for gi in range(4):
            bank, bn = (psT, "psT") if gi % 2 == 0 else (psU, "psU")
            twb(bn)
            for kb in range(8):
                col = gi * 8 + kb
                mm(bank[:, kb * 128:(kb + 1) * 128],
                   xln_bf[:, col * 128:(col + 1) * 128], id_sb[:, :],
                   True, True, tr=True, pe=(kb == 7))
            W("v", "PE", cv("PE"))
            dst = xlnT[:, t * 4096 + gi * 1024: t * 4096 + (gi + 1) * 1024]
            E("v", lambda e, d=dst, b=bank: e.tensor_copy(d, b[:, :])
              .then_inc(SM["VE"], 1))
            bank_free[bn] = ("VE", inc("VE"))
        xln_free_pe[0] = cv("PE")
        xlnT_ve[t] = cv("VE")

    def emit_qkv(L):
        robuf_free = [0, 0]
        stg_cnt = {"q": [0, 0], "k": [0, 0], "v": [0, 0]}
        W("y", "CC", cv("CC"))
        for n in ("szs0", "szs1", "szl0", "szl1"):
            W("v", n, cv(n))
        for s in range(12):
            load_wstrip(qkvw[L, s, :, 0:16, :], 0)
            load_wstrip(qkvw[L, s, :, 16:32, :], 1)
            W("t", "sw0", cv("sw0"))
            W("t", "sw1", cv("sw1"))
            pe_last = 0
            for t in range(NT):
                bank, bn = (psA, "psA") if t % 2 == 0 else (psB, "psB")
                twb(bn)
                W("t", "VE", xlnT_ve[t])
                for kt in range(32):
                    mm(bank[:, :],
                       xlnT[:, t * 4096 + kt * 128: t * 4096 + (kt + 1) * 128],
                       wb[kt // 16][:, (kt % 16) * 512:((kt % 16) + 1) * 512],
                       start=(kt == 0), stop=(kt == 31), pe=(kt == 31))
                pe_last = cv("PE")
                if s < 10:
                    W("v", "PE", pe_last)
                    rp = t % 2
                    W("v", "PE", robuf_free[rp])
                    ct = cos_sb[:, t * 64:(t + 1) * 64]
                    st_ = sin_sb[:, t * 64:(t + 1) * 64]
                    for i in range(4):
                        b0 = i * 128
                        E("v", lambda e, b0=b0, bank=bank, ct=ct:
                          e.tensor_mul(t1f[:, 0:64], bank[:, b0:b0 + 64], ct))
                        E("v", lambda e, b0=b0, bank=bank, ct=ct:
                          e.tensor_mul(t1f[:, 64:128],
                                       bank[:, b0 + 64:b0 + 128], ct))
                        E("v", lambda e, b0=b0, bank=bank, st_=st_:
                          e.tensor_mul(t2f[:, 0:64],
                                       bank[:, b0 + 64:b0 + 128], st_))
                        E("v", lambda e, b0=b0, bank=bank, st_=st_:
                          e.tensor_mul(t2f[:, 64:128], bank[:, b0:b0 + 64], st_))
                        E("v", lambda e, b0=b0, rp=rp:
                          e.tensor_sub(robuf[rp][:, b0:b0 + 64],
                                       t1f[:, 0:64], t2f[:, 0:64]))
                        if i == 3:
                            E("v", lambda e, b0=b0, rp=rp:
                              e.tensor_add(robuf[rp][:, b0 + 64:b0 + 128],
                                           t1f[:, 64:128], t2f[:, 64:128])
                              .then_inc(SM["VE"], 1))
                        else:
                            E("v", lambda e, b0=b0, rp=rp:
                              e.tensor_add(robuf[rp][:, b0 + 64:b0 + 128],
                                           t1f[:, 64:128], t2f[:, 64:128]))
                    bank_free[bn] = ("VE", inc("VE"))
                    W("t", "VE", cv("VE"))
                    bank2, bn2 = (psT, "psT") if t % 2 == 0 else (psU, "psU")
                    twb(bn2)
                    for i in range(4):
                        mm(bank2[:, i * 128:(i + 1) * 128],
                           robuf[rp][:, i * 128:(i + 1) * 128], id_sb[:, :],
                           True, True, tr=True, pe=(i == 3))
                    robuf_free[rp] = cv("PE")
                    kind = "q" if s < 8 else "k"
                    stg = qstage if kind == "q" else kstage
                    sn = ("sqs%d" if kind == "q" else "sks%d")
                    c = stg_cnt[kind]
                    p = (c[0] + c[1]) % 2
                    W("v", sn % p, 16 * c[p])
                    W("v", "PE", cv("PE"))
                    E("v", lambda e, p=p, b=bank2, stg=stg:
                      e.tensor_copy(stg[p][:, :], b[:, 0:512])
                      .then_inc(SM["VE"], 1))
                    bank_free[bn2] = ("VE", inc("VE"))
                    W("y", "VE", cv("VE"))
                    if kind == "q":
                        dst = qtp[t, 4 * s:4 * s + 4].transpose([1, 0, 2])
                    else:
                        g0 = 4 * (s - 8)
                        dst = agi_kt[g0:g0 + 4, :, t, :].transpose([1, 0, 2])
                    E("y", lambda e, d=dst, p=p, stg=stg, sm=SM[sn % p]:
                      e.dma_start(out=d, in_=stg[p][:, :]).then_inc(sm, 16))
                    c[p] += 1
                    inc(sn % p, 16)
                else:
                    c = stg_cnt["v"]
                    p = (c[0] + c[1]) % 2
                    W("v", "svs%d" % p, 16 * c[p])
                    W("v", "PE", pe_last)
                    E("v", lambda e, p=p, bank=bank:
                      e.tensor_copy(vstage[p][:, :], bank[:, :])
                      .then_inc(SM["VE"], 1))
                    bank_free[bn] = ("VE", inc("VE"))
                    W("y", "VE", cv("VE"))
                    dst = agi_v[t, :, (s - 10) * 512:(s - 10) * 512 + 512]
                    E("y", lambda e, d=dst, p=p, sm=SM["svs%d" % p]:
                      e.dma_start(out=d, in_=vstage[p][:, :]).then_inc(sm, 16))
                    c[p] += 1
                    inc("svs%d" % p, 16)
            wb_free_pe[0] = pe_last
            wb_free_pe[1] = pe_last

    def emit_ag():
        for n in ["sks0", "sks1", "svs0", "svs1", "skv0", "skv1"]:
            W("g", n, cv(n))
        E("g", lambda e: e.collective_compute(
            "AllGather", ALU.bypass,
            replica_groups=[[0, 1, 2, 3], [4, 5, 6, 7]],
            ins=[ag_in.opt()], outs=[ag_out_t.ap().opt()])
          .then_inc(SM["CC"], 1))
        inc("CC")

    last_z_ve = [0, 0, 0, 0]
    kv_free = [0, 0]
    qt_free = [0, 0]

    def emit_attention(L, grp):
        for t in range(4 * grp, 4 * grp + 4):
            tm = t % 4
            nch = t + 1
            for g in range(8):
                gp = g % 2
                W("y", "CC", cv("CC"))
                W("y", "PE", kv_free[gp])
                for kc in range(nch):
                    src = ago_kt[:, g, :, kc, :].transpose([1, 0, 2])
                    E("y", lambda e, s_=src, gp=gp, kc=kc, sm=SM["skv%d" % gp]:
                      e.dma_start(out=ktg[gp][:, kc * 512:(kc + 1) * 512], in_=s_)
                      .then_inc(sm, 16))
                    inc("skv%d" % gp, 16)
                    srcv = ago_v[:, kc, :, g * 128:(g + 1) * 128].transpose([1, 0, 2])
                    E("y", lambda e, s_=srcv, gp=gp, kc=kc, sm=SM["skv%d" % gp]:
                      e.dma_start(out=vg[gp][:, kc * 512:(kc + 1) * 512], in_=s_)
                      .then_inc(sm, 16))
                    inc("skv%d" % gp, 16)
                W("y", "PE", qt_free[gp])
                E("y", lambda e, t=t, g=g, gp=gp, sm=SM["sqt%d" % gp]:
                  e.dma_start(out=qtq[gp][:, :],
                              in_=qtp[t, 4 * g:4 * g + 4].transpose([1, 0, 2]))
                  .then_inc(sm, 16))
                inc("sqt%d" % gp, 16)

                se_exp = [[0] * nch for _ in range(4)]

                def emit_scores(i):
                    if i == 0:
                        W("t", "skv%d" % gp, cv("skv%d" % gp))
                        W("t", "sqt%d" % gp, cv("sqt%d" % gp))
                    W("s", "VE", last_z_ve[i])
                    for kc in range(nch):
                        bn = "psS%d" % (kc % 2)
                        bank = psS[kc % 2]
                        twb(bn)
                        diag = (kc == t)
                        mm(bank[:, :], qtq[gp][:, i * 128:(i + 1) * 128],
                           ktg[gp][:, kc * 512:(kc + 1) * 512],
                           start=True, stop=not diag, pe=not diag)
                        if diag:
                            mm(bank[:, :], atri_sb[:, :], mmask_sb[:, :],
                               start=False, stop=True, pe=True)
                        W("s", "PE", cv("PE"))
                        E("s", lambda e, i=i, kc=kc, bank=bank:
                          e.activation(Pst[i % 2][:, kc * 512:(kc + 1) * 512],
                                       bank[:, :], AF.Exp, bias=EBIAS,
                                       scale=SCALE,
                                       accum_out=zc[:, i * 8 + kc:i * 8 + kc + 1])
                          .then_inc(SM["SE"], 1))
                        se_exp[i][kc] = inc("SE")
                        bank_free[bn] = ("SE", se_exp[i][kc])
                    W("v", "SE", se_exp[i][nch - 1])
                    E("v", lambda e, i=i, nch=nch: e.reduce_sum(
                        colz[:, 8:9], zc[:, i * 8:i * 8 + nch], axis=AX))
                    E("v", lambda e: e.tensor_copy(colz[:, 2:3], colz[:, 2:3]))
                    E("v", lambda e: e.tensor_copy(colz[:, 2:3], colz[:, 2:3]))
                    E("v", lambda e, i=i: e.reciprocal(rinvq[:, i:i + 1],
                                                       colz[:, 8:9])
                      .then_inc(SM["VE"], 1))
                    last_z_ve[i] = inc("VE")

                def emit_ptav(i, first_av, last_av):
                    nblk = 4 * nch
                    for kb0 in range(0, nblk, 8):
                        kbn = min(8, nblk - kb0)
                        bank2, bn2 = ((psT, "psT") if (kb0 // 8) % 2 == 0
                                      else (psU, "psU"))
                        twb(bn2)
                        W("t", "SE", se_exp[i][min(nch - 1, (kb0 + kbn - 1) // 4)])
                        for kk in range(kbn):
                            kb = kb0 + kk
                            mm(bank2[:, kk * 128:(kk + 1) * 128],
                               Pst[i % 2][:, kb * 128:(kb + 1) * 128],
                               id_sb[:, :], True, True, tr=True,
                               pe=(kk == kbn - 1))
                        W("v", "PE", cv("PE"))
                        E("v", lambda e, kb0=kb0, kbn=kbn, b=bank2:
                          e.tensor_copy(pt_sb[:, kb0 * 128:(kb0 + kbn) * 128],
                                        b[:, 0:kbn * 128]).then_inc(SM["VE"], 1))
                        bank_free[bn2] = ("VE", inc("VE"))
                    W("t", "VE", cv("VE"))
                    if first_av:
                        eng, c = bank_free["psO"]
                        W("t", eng, c)
                    for kb in range(nblk):
                        mm(psO[:, i * 128:(i + 1) * 128],
                           pt_sb[:, kb * 128:(kb + 1) * 128],
                           vg[gp][:, kb * 128:(kb + 1) * 128],
                           start=(first_av and kb == 0),
                           stop=(last_av and kb == nblk - 1),
                           pe=(kb == nblk - 1))

                emit_scores(0)
                for i in range(1, 4):
                    emit_scores(i)
                    emit_ptav(i - 1, first_av=(i == 1), last_av=False)
                emit_ptav(3, first_av=False, last_av=True)
                qt_free[gp] = cv("PE")
                kv_free[gp] = cv("PE")
                W("v", "PE", cv("PE"))
                for i in range(4):
                    last = (i == 3)
                    if last:
                        E("v", lambda e, i=i: e.tensor_scalar(
                            ostage[:, i * 128:(i + 1) * 128],
                            psO[:, i * 128:(i + 1) * 128],
                            rinvq[:, i:i + 1], None, ALU.mult)
                          .then_inc(SM["VE"], 1))
                    else:
                        E("v", lambda e, i=i: e.tensor_scalar(
                            ostage[:, i * 128:(i + 1) * 128],
                            psO[:, i * 128:(i + 1) * 128],
                            rinvq[:, i:i + 1], None, ALU.mult))
                bank_free["psO"] = ("VE", inc("VE"))
                W("t", "VE", cv("VE"))
                twb("psT")
                for i in range(4):
                    mm(psT[:, i * 128:(i + 1) * 128],
                       ostage[:, i * 128:(i + 1) * 128], id_sb[:, :],
                       True, True, tr=True, pe=(i == 3))
                W("v", "PE", cv("PE"))
                E("v", lambda e, tm=tm, g=g: e.tensor_copy(
                    ot4[:, tm * 4096 + g * 512: tm * 4096 + (g + 1) * 512],
                    psT[:, 0:512]).then_inc(SM["VE"], 1))
                bank_free["psT"] = ("VE", inc("VE"))

    def emit_wo(L, grp):
        for s in range(8):
            load_wstrip(wow[L, s, :, 0:16, :], 0)
            load_wstrip(wow[L, s, :, 16:32, :], 1)
            W("t", "sw0", cv("sw0"))
            W("t", "sw1", cv("sw1"))
            W("t", "VE", cv("VE"))
            pe_last = 0
            for t in range(4 * grp, 4 * grp + 4):
                tm = t % 4
                bank, bn = (psA, "psA") if t % 2 == 0 else (psB, "psB")
                twb(bn)
                for kb in range(32):
                    mm(bank[:, :],
                       ot4[:, tm * 4096 + kb * 128: tm * 4096 + (kb + 1) * 128],
                       wb[kb // 16][:, (kb % 16) * 512:((kb % 16) + 1) * 512],
                       start=(kb == 0), stop=(kb == 31), pe=(kb == 31))
                pe_last = cv("PE")
                W("v", "PE", pe_last)
                xs = Xsb[:, t * 4096 + s * 512: t * 4096 + (s + 1) * 512]
                E("v", lambda e, xs=xs, bank=bank:
                  e.tensor_add(xs, xs, bank[:, :]).then_inc(SM["VE"], 1))
                bank_free[bn] = ("VE", inc("VE"))
                if s == 7:
                    x_ready_ve[t] = cv("VE")
            wb_free_pe[0] = pe_last
            wb_free_pe[1] = pe_last

    def emit_ut(t, scale_pcol):
        twb("psX")
        W("t", "VE", xlnT_ve[t])
        W("t", "scl2", cv("scl2"))
        for kt in range(32):
            mm(psX[:, 0:128],
               xlnT[:, t * 4096 + kt * 128: t * 4096 + (kt + 1) * 128],
               acat_sb[:, kt * 128:(kt + 1) * 128],
               start=(kt == 0), stop=(kt == 31), pe=(kt == 31))
        W("v", "PE", cv("PE"))
        E("v", lambda e: e.tensor_copy(ustage[:, :], psX[:, 0:128])
          .then_inc(SM["VE"], 1))
        bank_free["psX"] = ("VE", inc("VE"))
        W("t", "VE", cv("VE"))
        twb("psT")
        mm(psT[:, 0:128], ustage[:, :], id_sb[:, :], True, True,
           tr=True, pe=True)
        W("v", "PE", cv("PE"))
        if scale_pcol:
            E("v", lambda e, t=t: e.tensor_scalar(
                ut_sb[:, t * 128:(t + 1) * 128], psT[:, 0:128],
                pcol_sb[:, 0:1], None, ALU.mult).then_inc(SM["VE"], 1))
        else:
            E("v", lambda e: e.tensor_copy(htmp[:, :], psT[:, 0:128])
              .then_inc(SM["VE"], 1))
        bank_free["psT"] = ("VE", inc("VE"))

    def emit_ffn0():
        for t in range(NT):
            emit_layernorm(t, first=False)
            emit_ut(t, scale_pcol=True)
        zs_cnt = [0, 0]
        W("s", "svs0", cv("svs0"))
        W("s", "svs1", cv("svs1"))
        W("y", "sks0", cv("sks0"))
        W("y", "sks1", cv("sks1"))
        for s in range(8):
            load_wstrip(ffnw[0, s, :, 0:16, :], 0)
            load_wstrip(ffnw[0, s, :, 16:32, :], 1)
            load_bias_slice(0, s)
            W("t", "sw0", cv("sw0"))
            W("t", "sw1", cv("sw1"))
            W("t", "scl2", cv("scl2"))
            pe_last = 0
            for t in range(NT):
                bank, bn = (psA, "psA") if t % 2 == 0 else (psB, "psB")
                twb(bn)
                for kt in range(32):
                    mm(bank[:, :],
                       xlnT[:, t * 4096 + kt * 128: t * 4096 + (kt + 1) * 128],
                       wb[kt // 16][:, (kt % 16) * 512:((kt % 16) + 1) * 512],
                       start=(kt == 0), stop=False)
                mm(bank[:, :], ones_sb[:, :], bias_sb[:, :],
                   start=False, stop=True, pe=True)
                pe_last = cv("PE")
                bias_free_pe[0] = pe_last
                p = (zs_cnt[0] + zs_cnt[1]) % 2
                W("s", "szs%d" % p, 16 * zs_cnt[p])
                W("s", "PE", pe_last)
                E("s", lambda e, p=p, bank=bank:
                  e.copy(zstage[p][:, :], bank[:, :]).then_inc(SM["SE"], 1))
                se_z = inc("SE")
                E("s", lambda e, t=t, s=s, bank=bank: e.activation(
                    junkb[:, :], bank[:, :], AF.Square,
                    accum_out=szc[:, t * 8 + s: t * 8 + s + 1])
                  .then_inc(SM["SE"], 1))
                bank_free[bn] = ("SE", inc("SE"))
                W("y", "SE", se_z)
                E("y", lambda e, t=t, s=s, p=p, sm=SM["szs%d" % p]:
                  e.dma_start(out=zpark[t, :, s * 512:(s + 1) * 512],
                              in_=zstage[p][:, :]).then_inc(sm, 16))
                zs_cnt[p] += 1
                inc("szs%d" % p, 16)
            wb_free_pe[0] = pe_last
            wb_free_pe[1] = pe_last
        W("t", "VE", cv("VE"))
        for t in range(NT):
            for s in range(8):
                bank, bn = (psA, "psA") if s % 2 == 0 else (psB, "psB")
                twb(bn)
                mm(bank[:, :], ut_sb[:, t * 128:(t + 1) * 128],
                   bcat_sb[:, s * 512:(s + 1) * 512], True, True, pe=True)
                W("s", "PE", cv("PE"))
                E("s", lambda e, t=t, s=s, bank=bank: e.activation(
                    junkb[:, :], bank[:, :], AF.Square,
                    accum_out=smc[:, t * 8 + s: t * 8 + s + 1])
                  .then_inc(SM["SE"], 1))
                bank_free[bn] = ("SE", inc("SE"))
        W("v", "SE", cv("SE"))
        for t in range(NT):
            E("v", lambda e, t=t: e.reduce_sum(zc[:, t:t + 1],
                                               szc[:, t * 8:t * 8 + 8], axis=AX))
        for t in range(NT):
            E("v", lambda e, t=t: e.reduce_sum(zc[:, 8 + t:9 + t],
                                               smc[:, t * 8:t * 8 + 8], axis=AX))
        E("v", lambda e: e.tensor_copy(colz[:, 2:3], colz[:, 2:3])
          .then_inc(SM["VE"], 1))
        inc("VE")
        W("s", "VE", cv("VE"))
        E("s", lambda e: e.activation(zc[:, 16:24], zc[:, 0:8], AF.Sqrt))
        E("s", lambda e: e.activation(zc[:, 24:32], zc[:, 8:16], AF.Sqrt)
          .then_inc(SM["SE"], 1))
        inc("SE")
        W("v", "SE", cv("SE"))
        E("v", lambda e: e.tensor_scalar(szc[:, 0:8], zc[:, 24:32],
                                         cims[:, 3:4], None, ALU.add))
        E("v", lambda e: e.tensor_copy(colz[:, 2:3], colz[:, 2:3]))
        E("v", lambda e: e.tensor_copy(colz[:, 2:3], colz[:, 2:3]))
        E("v", lambda e: e.reciprocal(szc[:, 8:16], szc[:, 0:8]))
        E("v", lambda e: e.tensor_copy(colz[:, 2:3], colz[:, 2:3]))
        E("v", lambda e: e.tensor_copy(colz[:, 2:3], colz[:, 2:3]))
        E("v", lambda e: e.tensor_mul(szc[:, 16:24], zc[:, 16:24],
                                      szc[:, 8:16]))
        E("v", lambda e: e.tensor_copy(colz[:, 2:3], colz[:, 2:3]))
        E("v", lambda e: e.tensor_copy(colz[:, 2:3], colz[:, 2:3]))
        E("v", lambda e: e.tensor_scalar(gam[:, 0:8], szc[:, 16:24],
                                         cims[:, 1:2], cims[:, 2:3],
                                         ALU.mult, ALU.min))
        E("v", lambda e: e.tensor_copy(colz[:, 2:3], colz[:, 2:3]))
        E("v", lambda e: e.tensor_copy(colz[:, 2:3], colz[:, 2:3])
          .then_inc(SM["VE"], 1))
        inc("VE")
        ve_gam = cv("VE")
        W("y", "szs0", cv("szs0"))
        W("y", "szs1", cv("szs1"))
        zl_cnt = [0, 0]
        zld_free = [ve_gam, ve_gam]
        for t in range(NT):
            for s in range(8):
                p = (zl_cnt[0] + zl_cnt[1]) % 2
                W("y", "VE", zld_free[p])
                E("y", lambda e, t=t, s=s, p=p, sm=SM["szl%d" % p]:
                  e.dma_start(out=zld[p][:, :],
                              in_=zpark[t, :, s * 512:(s + 1) * 512])
                  .then_inc(sm, 16))
                zl_cnt[p] += 1
                inc("szl%d" % p, 16)
                bank, bn = (psA, "psA") if s % 2 == 0 else (psB, "psB")
                twb(bn)
                mm(bank[:, :], ut_sb[:, t * 128:(t + 1) * 128],
                   bcat_sb[:, s * 512:(s + 1) * 512], True, True, pe=True)
                W("v", "PE", cv("PE"))
                W("v", "VE", ve_gam)
                W("v", "szl%d" % p, cv("szl%d" % p))
                xs = Xsb[:, t * 4096 + s * 512: t * 4096 + (s + 1) * 512]
                E("v", lambda e, xs=xs, p=p: e.tensor_add(xs, xs, zld[p][:, :]))
                E("v", lambda e, t=t, xs=xs, bank=bank:
                  e.scalar_tensor_tensor(xs, bank[:, :], gam[:, t:t + 1],
                                         xs, ALU.mult, ALU.add)
                  .then_inc(SM["VE"], 1))
                bank_free[bn] = ("VE", inc("VE"))
                zld_free[p] = cv("VE")
                if s == 7:
                    x_ready_ve[t] = cv("VE")

    def emit_ffn1():
        pr_free_ve = 0
        for t in range(NT):
            emit_layernorm(t, first=False)
            emit_ut(t, scale_pcol=False)
            twb("psX")
            for kt in range(32):
                mm(psX[:, 0:15],
                   xlnT[:, t * 4096 + kt * 128: t * 4096 + (kt + 1) * 128],
                   rwt_sb[:, kt * 15:(kt + 1) * 15],
                   start=(kt == 0), stop=False)
            mm(psX[:, 0:15], ones_sb[:, :], rb_sb[:, :], start=False,
               stop=True, pe=True)
            W("s", "PE", cv("PE"))
            W("s", "VE", pr_free_ve)
            E("s", lambda e: e.activation(pr_sb[:, 0:15], psX[:, 0:15], AF.Exp,
                                          accum_out=rcol[:, 0:1])
              .then_inc(SM["SE"], 1))
            bank_free["psX"] = ("SE", inc("SE"))
            W("v", "SE", cv("SE"))
            E("v", lambda e: e.reciprocal(rcol[:, 1:2], rcol[:, 0:1]))
            E("v", lambda e: e.memset(rnk[:, 16:17], 0.0))
            E("v", lambda e: e.tensor_copy(colz[:, 2:3], colz[:, 2:3]))
            E("v", lambda e: e.tensor_scalar(pr_sb[:, 0:15], pr_sb[:, 0:15],
                                             rcol[:, 1:2], None, ALU.mult))
            E("v", lambda e: e.tensor_copy(colz[:, 2:3], colz[:, 2:3]))
            E("v", lambda e: e.tensor_copy(colz[:, 2:3], colz[:, 2:3]))
            for n in range(15):
                E("v", lambda e, n=n: e.tensor_scalar(
                    junk15[:, 0:15], pr_sb[:, 0:15], pr_sb[:, n:n + 1], None,
                    ALU.is_gt, ALU.add, accum_out=rnk[:, n:n + 1]))
            for n in range(1, 15):
                E("v", lambda e, n=n: e.tensor_scalar(
                    junk15[:, 0:n], pr_sb[:, 0:n], pr_sb[:, n:n + 1], None,
                    ALU.is_equal, ALU.add, accum_out=rnk[:, 16 + n:17 + n]))
            E("v", lambda e: e.tensor_copy(colz[:, 2:3], colz[:, 2:3]))
            E("v", lambda e: e.tensor_copy(colz[:, 2:3], colz[:, 2:3]))
            E("v", lambda e: e.tensor_add(rnk[:, 0:15], rnk[:, 0:15],
                                          rnk[:, 16:31]))
            E("v", lambda e: e.tensor_copy(colz[:, 2:3], colz[:, 2:3]))
            E("v", lambda e: e.tensor_copy(colz[:, 2:3], colz[:, 2:3]))
            E("v", lambda e: e.tensor_scalar(rnk[:, 16:31], rnk[:, 0:15],
                                             cims[:, 4:5], None, ALU.is_lt))
            E("v", lambda e: e.tensor_copy(colz[:, 2:3], colz[:, 2:3]))
            E("v", lambda e: e.tensor_copy(colz[:, 2:3], colz[:, 2:3]))
            E("v", lambda e: e.tensor_mul(psel[:, 0:15], pr_sb[:, 0:15],
                                          rnk[:, 16:31]))
            E("v", lambda e: e.memset(psel[:, 15:16], 0.0)
              .then_inc(SM["VE"], 1))
            pr_free_ve = inc("VE")
            W("t", "VE", cv("VE"))
            twb("psU")
            mm(psU[0:16, 0:128], psel[:, 0:16], id_sb[:, :], True, True,
               tr=True, pe=True)
            W("v", "PE", cv("PE"))
            E("v", lambda e: e.tensor_copy(pselT[:, :], psU[0:15, 0:128])
              .then_inc(SM["VE"], 1))
            bank_free["psU"] = ("VE", inc("VE"))
            W("t", "VE", cv("VE"))
            twb("psX")
            mm(psX[0:8, 0:128], msb[:, :], pselT[:, :], True, True, pe=True)
            W("v", "PE", cv("PE"))
            E("v", lambda e: e.tensor_copy(wt_bf[:, :], psX[0:8, 0:128])
              .then_inc(SM["VE"], 1))
            bank_free["psX"] = ("VE", inc("VE"))
            W("t", "VE", cv("VE"))
            twb("psX")
            mm(psX[:, 128:256], esb[:, :], wt_bf[:, :], True, True, pe=True)
            W("v", "PE", cv("PE"))
            E("v", lambda e: e.tensor_copy(wf_bf[:, :], psX[:, 128:256])
              .then_inc(SM["VE"], 1))
            bank_free["psX"] = ("VE", inc("VE"))
            E("v", lambda e: e.tensor_copy(colz[:, 2:3], colz[:, 2:3]))
            E("v", lambda e: e.tensor_copy(colz[:, 2:3], colz[:, 2:3]))
            E("v", lambda e, t=t: e.tensor_mul(ut_sb[:, t * 128:(t + 1) * 128],
                                               htmp[:, :], wf_bf[:, :])
              .then_inc(SM["VE"], 1))
            inc("VE")
        for s in range(8):
            load_wstrip(ffnw[1, s, :, 0:16, :], 0)
            load_wstrip(ffnw[1, s, :, 16:32, :], 1)
            load_bias_slice(1, s)
            W("t", "sw0", cv("sw0"))
            W("t", "sw1", cv("sw1"))
            W("t", "scl2", cv("scl2"))
            W("t", "VE", cv("VE"))
            pe_last = 0
            for t in range(NT):
                bank, bn = (psA, "psA") if t % 2 == 0 else (psB, "psB")
                twb(bn)
                for kt in range(32):
                    mm(bank[:, :],
                       xlnT[:, t * 4096 + kt * 128: t * 4096 + (kt + 1) * 128],
                       wb[kt // 16][:, (kt % 16) * 512:((kt % 16) + 1) * 512],
                       start=(kt == 0), stop=False)
                mm(bank[:, :], ones_sb[:, :], bias_sb[:, :],
                   start=False, stop=False)
                mm(bank[:, :], ut_sb[:, t * 128:(t + 1) * 128],
                   bcat_sb[:, s * 512:(s + 1) * 512], start=False, stop=True,
                   pe=True)
                pe_last = cv("PE")
                bias_free_pe[0] = pe_last
                W("v", "PE", pe_last)
                xs = Xsb[:, t * 4096 + s * 512: t * 4096 + (s + 1) * 512]
                E("v", lambda e, xs=xs, bank=bank:
                  e.tensor_add(xs, xs, bank[:, :]).then_inc(SM["VE"], 1))
                bank_free[bn] = ("VE", inc("VE"))
                if s == 7:
                    x_ready_ve[t] = cv("VE")
                    W("g", "VE", cv("VE"))
                    E("g", lambda e, t=t: e.dma_start(
                        out=y[t * 128:(t + 1) * 128, :],
                        in_=Xsb[:, t * 4096:(t + 1) * 4096])
                      .then_inc(SM["sys"], 16))
                    inc("sys", 16)
            wb_free_pe[0] = pe_last
            wb_free_pe[1] = pe_last

    def load_layer_consts(L):
        W("g", "PE", cv("PE"))
        W("g", "SE", cv("SE"))
        pairs = [(acat_sb, acat[L].rearrange("p k e -> p (k e)")),
                 (bcat_sb, bcat[L])]
        if L == 1:
            pairs.append((rwt_sb, rwt.rearrange("p k n -> p (k n)")))
        for dst, src in pairs:
            E("g", lambda e, d=dst, s=src: e.dma_start(out=d[:, :], in_=s)
              .then_inc(SM["scl2"], 16))
            inc("scl2", 16)

    for L in range(2):
        phase_barrier()
        for t in range(NT):
            emit_layernorm(t, first=(L == 0))
        W("t", "scl", cv("scl"))
        emit_qkv(L)
        emit_ag()
        phase_barrier()
        for grp in range(2):
            emit_attention(L, grp)
            emit_wo(L, grp)
        load_layer_consts(L)
        phase_barrier()
        if L == 0:
            emit_ffn0()
        else:
            emit_ffn1()
        phase_barrier()
    W("g", "sys", cv("sys"))

    with nc.Block() as blk:
        @blk.tensor
        def _(e):
            for f in prog["t"]:
                f(e)

        @blk.vector
        def _(e):
            for f in prog["v"]:
                f(e)

        @blk.scalar
        def _(e):
            for f in prog["s"]:
                f(e)

        @blk.gpsimd
        def _(e):
            for f in prog["g"]:
                f(e)

        @blk.sync
        def _(e):
            for f in prog["y"]:
                f(e)

    return nc


def _tile_w(w):
    Din, Dout = w.shape
    ns, kt = Dout // 512, Din // 128
    return np.ascontiguousarray(w.reshape(kt, 128, ns, 512).transpose(2, 1, 0, 3))


_NC_CACHE = [None]


def kernel(L0_attn_x, L0_attn_Wq, L0_attn_Wk, L0_attn_Wv, L0_attn_Wo,
           L0_ffn_W, L0_ffn_b, L0_ffn_A, L0_ffn_B, L0_ffn_p_scores,
           L1_attn_w_qkv, L1_attn_w_o, L1_ffn_base_W, L1_ffn_base_b,
           L1_ffn_router_W, L1_ffn_router_b, L1_ffn_A, L1_ffn_B,
           L1_ffn_expert_map):
    x = np.asarray(L0_attn_x, dtype=np.float32)

    qkv0 = np.concatenate([np.asarray(L0_attn_Wq, np.float32),
                           np.asarray(L0_attn_Wk, np.float32),
                           np.asarray(L0_attn_Wv, np.float32)], axis=1)
    wq1 = np.array(L1_attn_w_qkv, dtype=np.float32)
    perm = np.arange(128).reshape(64, 2).T.reshape(-1)
    for h in range(32):
        wq1[:, h * 128:(h + 1) * 128] = wq1[:, h * 128 + perm]
    for g in range(8):
        b0 = 4096 + g * 128
        wq1[:, b0:b0 + 128] = wq1[:, b0 + perm]
    qkvw = np.stack([_tile_w(qkv0), _tile_w(wq1)])
    wow = np.stack([_tile_w(np.asarray(L0_attn_Wo, np.float32)),
                    _tile_w(np.asarray(L1_attn_w_o, np.float32))])
    ffnw = np.stack([
        _tile_w(np.ascontiguousarray(np.asarray(L0_ffn_W, np.float32).T)),
        _tile_w(np.ascontiguousarray(np.asarray(L1_ffn_base_W, np.float32).T))])
    ac0 = np.asarray(L0_ffn_A, np.float32).transpose(1, 0, 2).reshape(4096, 128)
    ac1 = np.asarray(L1_ffn_A, np.float32).transpose(2, 0, 1).reshape(4096, 128)
    acat = np.stack([np.ascontiguousarray(a.reshape(32, 128, 128)
                                          .transpose(1, 0, 2))
                     for a in (ac0, ac1)])
    bcat = np.stack([np.asarray(L0_ffn_B, np.float32).reshape(128, 4096),
                     np.ascontiguousarray(np.asarray(L1_ffn_B, np.float32)
                                          .transpose(0, 2, 1)
                                          .reshape(128, 4096))])
    rwt = np.ascontiguousarray(np.asarray(L1_ffn_router_W, np.float32).T
                               .reshape(32, 128, 15).transpose(1, 0, 2))
    biasv = np.stack([np.asarray(L0_ffn_b, np.float32).reshape(1, 4096),
                      np.asarray(L1_ffn_base_b, np.float32).reshape(1, 4096)])
    rbias = np.asarray(L1_ffn_router_b, np.float32).reshape(1, 15)
    pcolv = np.repeat(np.asarray(L0_ffn_p_scores, np.float32), 16).reshape(128, 1)
    emap = np.asarray(L1_ffn_expert_map, np.int32).reshape(15, 1)

    identc = np.eye(128, dtype=np.float32).astype(BF)
    dd = np.arange(128)
    atric = (dd[:, None] >= dd[None, :]).astype(np.float32).astype(BF)
    onesc = np.ones((1, 128), np.float32).astype(BF)
    esbc = (2.0 * (np.arange(128)[None, :] // 16 == np.arange(8)[:, None])
            ).astype(np.float32).astype(BF)
    inv = (1.0 / (np.float32(10000.0) **
                  (np.arange(0, 128, 2, dtype=np.float32) / np.float32(128.0))
                  )).astype(np.float32)
    cims_h = np.ascontiguousarray(np.tile(np.array(
        [1.0 / 4096.0, 0.5, 1.0, 1e-6, 7.5, 0, 1, 2, 3, 4, 5, 6, 7],
        np.float32), (128, 1)))

    in_maps = []
    shard_blocks = []
    for core in range(8):
        b, j = divmod(core, 4)
        blocks = np.arange(j, 32, 4)
        shard_blocks.append((b, blocks))
        xs = np.ascontiguousarray(
            x[b].reshape(32, 128, 4096)[blocks].reshape(1024, 4096))
        pos = (blocks[:, None] * 128 + np.arange(128)[None, :]).reshape(-1)
        ang = pos[:, None].astype(np.float32) * inv[None, :]
        c64 = np.cos(ang).astype(np.float32)
        s64 = np.sin(ang).astype(np.float32)
        cosc = np.ascontiguousarray(
            c64.reshape(8, 128, 64).transpose(1, 0, 2).reshape(128, 512))
        sincv = np.ascontiguousarray(
            s64.reshape(8, 128, 64).transpose(1, 0, 2).reshape(128, 512))
        kr = np.arange(512)[None, :]
        mmaskc = (np.float32(NEG) * (kr >= dd[:, None] + j * 128 + 1)
                  ).astype(np.float32).astype(BF)
        in_maps.append({
            "xsh": xs, "qkvw": qkvw, "wow": wow, "ffnw": ffnw, "acat": acat,
            "bcat": bcat, "rwt": rwt, "biasv": biasv, "rbias": rbias,
            "pcolv": pcolv, "emap": emap, "identc": identc, "atric": atric,
            "onesc": onesc, "esbc": esbc, "cosc": cosc, "sinc": sincv,
            "mmaskc": mmaskc, "cimsc": cims_h,
        })

    if _NC_CACHE[0] is None:
        _NC_CACHE[0] = build_nc()
    nc = _NC_CACHE[0]
    res = run_bass_kernel_spmd(nc, in_maps, core_ids=list(range(8)))

    out = np.empty((2, 4096, 4096), dtype=np.float32)
    ov = out.reshape(2, 32, 128, 4096)
    for core in range(8):
        b, blocks = shard_blocks[core]
        yc = np.asarray(res.results[core]["y"]).reshape(8, 128, 4096)
        ov[b, blocks] = yc
    return out


# version 09
# speedup vs baseline: 1.0290x; 1.0290x over previous
import numpy as np
import ml_dtypes
import concourse.bass as bass
import concourse.mybir as mybir
from concourse.bass_utils import run_bass_kernel_spmd

F32 = mybir.dt.float32
BF16 = mybir.dt.bfloat16
I32 = mybir.dt.int32
AF = mybir.ActivationFunctionType
ALU = mybir.AluOpType
AX = mybir.AxisListType.X
BF = ml_dtypes.bfloat16
f32 = np.float32

NT = 8
SCALE = f32(1.0 / np.sqrt(128.0))
EBIAS = f32(-15.0)
NEG = -1.0e30
AGZ = 1024 * 1024


def build_nc():
    nc = bass.Bass("TRN2", target_bir_lowering=False, debug=False)

    xsh = nc.dram_tensor("xsh", [1024, 4096], F32, kind="ExternalInput").ap()
    qkvw = nc.dram_tensor("qkvw", [2, 12, 128, 32, 512], F32, kind="ExternalInput").ap()
    wow = nc.dram_tensor("wow", [2, 8, 128, 32, 512], F32, kind="ExternalInput").ap()
    ffnw = nc.dram_tensor("ffnw", [2, 8, 128, 32, 512], F32, kind="ExternalInput").ap()
    acat = nc.dram_tensor("acat", [2, 128, 32, 128], F32, kind="ExternalInput").ap()
    bcat = nc.dram_tensor("bcat", [2, 128, 4096], F32, kind="ExternalInput").ap()
    rwt = nc.dram_tensor("rwt", [128, 32, 15], F32, kind="ExternalInput").ap()
    biasv = nc.dram_tensor("biasv", [2, 1, 4096], F32, kind="ExternalInput").ap()
    rbias = nc.dram_tensor("rbias", [1, 15], F32, kind="ExternalInput").ap()
    pcolv = nc.dram_tensor("pcolv", [128, 1], F32, kind="ExternalInput").ap()
    emap = nc.dram_tensor("emap", [15, 1], I32, kind="ExternalInput").ap()
    identc = nc.dram_tensor("identc", [128, 128], BF16, kind="ExternalInput").ap()
    atric = nc.dram_tensor("atric", [128, 128], BF16, kind="ExternalInput").ap()
    onesc = nc.dram_tensor("onesc", [1, 128], BF16, kind="ExternalInput").ap()
    esbc = nc.dram_tensor("esbc", [8, 128], BF16, kind="ExternalInput").ap()
    cosc = nc.dram_tensor("cosc", [128, 512], F32, kind="ExternalInput").ap()
    sinc = nc.dram_tensor("sinc", [128, 512], F32, kind="ExternalInput").ap()
    mmaskc = nc.dram_tensor("mmaskc", [128, 512], BF16, kind="ExternalInput").ap()
    cimsc = nc.dram_tensor("cimsc", [128, 13], F32, kind="ExternalInput").ap()
    y = nc.dram_tensor("y", [1024, 4096], F32, kind="ExternalOutput").ap()

    qtp = nc.dram_tensor("qtp", [8, 32, 128, 128], BF16, kind="Internal").ap()
    ag_in_t = nc.dram_tensor("ag_in", [2 * AGZ], BF16, kind="Internal")
    ag_out_t = nc.dram_tensor("ag_out", [8 * AGZ], BF16, kind="Internal")
    zpark = nc.dram_tensor("zpark", [8, 128, 4096], BF16, kind="Internal").ap()

    ag_in = ag_in_t.ap()
    agi_kt = ag_in[0:AGZ].rearrange("(g d t k) -> g d t k", g=8, d=128, t=8, k=128)
    agi_v = ag_in[AGZ:2 * AGZ].rearrange("(t k v) -> t k v", t=8, k=128, v=1024)
    ago5 = ag_out_t.ap().rearrange("(r s z) -> r s z", r=4, s=2, z=AGZ)
    ago_kt = ago5[:, 0, :].rearrange("r (g d t k) -> r g d t k", g=8, d=128, t=8, k=128)
    ago_v = ago5[:, 1, :].rearrange("r (t k v) -> r t k v", t=8, k=128, v=1024)

    A = nc.alloc_sbuf_tensor
    Xsb = A("Xsb", [128, 8 * 4096], BF16)
    cos_sb = A("cos_sb", [128, 512], F32)
    sin_sb = A("sin_sb", [128, 512], F32)
    id_sb = A("id_sb", [128, 128], BF16)
    atri_sb = A("atri_sb", [128, 128], BF16)
    mmask_sb = A("mmask_sb", [128, 512], BF16)
    ones_sb = A("ones_sb", [1, 128], BF16)
    esb = A("esb", [8, 128], BF16)
    pcol_sb = A("pcol_sb", [128, 1], F32)
    bias_sb = A("bias_sb", [1, 512], BF16)
    rb_sb = A("rb_sb", [1, 15], BF16)
    emap_i = A("emap_i", [15, 1], I32)
    emap_f = A("emap_f", [15, 1], F32)
    msb = A("msb", [15, 8], BF16)
    cims = A("cims", [128, 13], F32)
    wb = [A("wb0", [128, 16 * 512], BF16), A("wb1", [128, 16 * 512], BF16)]
    ut_sb = A("ut_sb", [128, 8 * 128], BF16)
    colz = A("colz", [128, 16], F32)
    szc = A("szc", [128, 64], F32)
    smc = A("smc", [128, 64], F32)
    gam = A("gam", [128, 8], F32)
    zc = A("zc", [128, 32], F32)
    rinvq = A("rinvq", [128, 4], F32)
    pr_sb = A("pr_sb", [128, 16], F32)
    psel = A("psel", [128, 16], BF16)
    junk15 = A("junk15", [128, 16], F32)
    rnk = A("rnk", [128, 32], F32)
    rcol = A("rcol", [128, 8], F32)
    pselT = A("pselT", [15, 128], BF16)
    wt_bf = A("wt_bf", [8, 128], BF16)
    junkb = A("junkb", [128, 512], BF16)
    t1f = A("t1f", [128, 128], F32)
    t2f = A("t2f", [128, 128], F32)
    robuf = [A("rob0", [128, 512], BF16), A("rob1", [128, 512], BF16)]
    qstage = [A("qst0", [128, 512], BF16), A("qst1", [128, 512], BF16)]
    kstage = [A("kst0", [128, 512], BF16), A("kst1", [128, 512], BF16)]
    vstage = [A("vst0", [128, 512], BF16), A("vst1", [128, 512], BF16)]
    zstage = vstage
    zld = kstage
    ostage = None
    qtq = [A("qtq0", [128, 512], BF16), A("qtq1", [128, 512], BF16)]
    ostage = robuf[0]
    htmp = robuf[1][:, 0:128]
    ustage = robuf[1][:, 128:256]
    wf_bf = robuf[1][:, 256:384]
    rwt_sb = A("rwt_sb", [128, 480], BF16)

    ARENA_COLS = 45056
    arena = A("arena", [128, ARENA_COLS], BF16)

    def av(off, cols):
        return arena[:, off:off + cols]

    xlnT = av(0, 32768)
    ot4 = av(0, 16384)
    ktg = [av(16384, 4096), av(20480, 4096)]
    vg = [av(24576, 4096), av(28672, 4096)]
    Pst = [av(32768, 4096), av(36864, 4096)]
    pt_sb = av(40960, 4096)
    acat_sb = av(32768, 4096)
    bcat_sb = av(36864, 4096)
    xln_bf = av(40960, 4096)

    psA = nc.alloc_psum_tensor("psA", [128, 512], F32)
    psB = nc.alloc_psum_tensor("psB", [128, 512], F32)
    psT = nc.alloc_psum_tensor("psT", [128, 1024], BF16)
    psU = nc.alloc_psum_tensor("psU", [128, 1024], BF16)
    psS = [nc.alloc_psum_tensor("psS0", [128, 512], F32),
           nc.alloc_psum_tensor("psS1", [128, 512], F32)]
    psO = nc.alloc_psum_tensor("psO", [128, 512], F32)
    psX = nc.alloc_psum_tensor("psX", [128, 512], F32)

    sem_names = ["PE", "VE", "SE", "CC",
                 "sxl", "scl", "scl2", "sw0", "sw1", "skv0", "skv1",
                 "sqt0", "sqt1", "sqs0", "sqs1", "sks0", "sks1",
                 "svs0", "svs1", "szs0", "szs1", "szl0", "szl1", "sys"]
    SM = {n: nc.alloc_semaphore(n) for n in sem_names}
    CNT = {n: 0 for n in sem_names}
    prog = {k: [] for k in "tvsgy"}

    def E(eng, fn):
        prog[eng].append(fn)

    def inc(n, k=1):
        CNT[n] += k
        return CNT[n]

    def cv(n):
        return CNT[n]

    def W(eng, sem, val):
        if val <= 0:
            return
        E(eng, lambda e, s=SM[sem], v=val: e.wait_ge(s, v))

    def mm(out, lhsT, rhs, start, stop, tr=False, pe=False):
        if pe:
            if tr:
                E("t", lambda e, o=out, l=lhsT, r=rhs, a=start, b=stop:
                  e.matmul(o, l, r, start=a, stop=b, is_transpose=True)
                  .then_inc(SM["PE"], 1))
            else:
                E("t", lambda e, o=out, l=lhsT, r=rhs, a=start, b=stop:
                  e.matmul(o, l, r, start=a, stop=b).then_inc(SM["PE"], 1))
            return inc("PE")
        if tr:
            E("t", lambda e, o=out, l=lhsT, r=rhs, a=start, b=stop:
              e.matmul(o, l, r, start=a, stop=b, is_transpose=True))
        else:
            E("t", lambda e, o=out, l=lhsT, r=rhs, a=start, b=stop:
              e.matmul(o, l, r, start=a, stop=b))
        return None

    bank_free = {"psA": ("VE", 0), "psB": ("VE", 0), "psT": ("VE", 0),
                 "psU": ("VE", 0), "psS0": ("SE", 0), "psS1": ("SE", 0),
                 "psO": ("VE", 0), "psX": ("VE", 0)}

    def twb(name):
        eng, c = bank_free[name]
        W("t", eng, c)

    def phase_barrier():
        pe0, ve0, se0 = cv("PE"), cv("VE"), cv("SE")
        W("t", "VE", ve0); W("t", "SE", se0)
        W("v", "PE", pe0); W("v", "SE", se0)
        W("s", "PE", pe0); W("s", "VE", ve0)
        W("y", "PE", pe0); W("y", "VE", ve0); W("y", "SE", se0)
        W("g", "PE", pe0); W("g", "VE", ve0); W("g", "SE", se0)

    wb_free_pe = [0, 0]

    def load_wstrip(src_ap, buf):
        W("g", "PE", wb_free_pe[buf])
        E("g", lambda e, a=src_ap, b=wb[buf], s=SM["sw%d" % buf]:
          e.dma_start(out=b[:, :], in_=a).then_inc(s, 16))
        inc("sw%d" % buf, 16)

    bias_free_pe = [0]

    def load_bias_slice(L, s):
        W("g", "PE", bias_free_pe[0])
        E("g", lambda e, L=L, s=s: e.dma_start(
            out=bias_sb[:, :], in_=biasv[L, :, s * 512:(s + 1) * 512])
          .then_inc(SM["scl2"], 16))
        inc("scl2", 16)

    for t in range(NT):
        E("g", lambda e, t=t: e.dma_start(
            out=Xsb[:, t * 4096:(t + 1) * 4096],
            in_=xsh[t * 128:(t + 1) * 128, :]).then_inc(SM["sxl"], 16))
        inc("sxl", 16)
    for dst, src in [(cos_sb, cosc), (sin_sb, sinc), (id_sb, identc),
                     (atri_sb, atric), (mmask_sb, mmaskc), (ones_sb, onesc),
                     (esb, esbc), (rb_sb, rbias), (pcol_sb, pcolv),
                     (emap_i, emap), (cims, cimsc)]:
        E("g", lambda e, d=dst, s=src: e.dma_start(out=d[:, :], in_=s)
          .then_inc(SM["scl"], 16))
        inc("scl", 16)
    W("s", "scl", cv("scl"))
    E("s", lambda e: e.copy(emap_f[:, :], emap_i[:, :]).then_inc(SM["SE"], 1))
    inc("SE")
    W("v", "SE", cv("SE"))
    for ei in range(8):
        E("v", lambda e, ei=ei: e.tensor_scalar(
            msb[:, ei:ei + 1], emap_f[:, :], cims[0:15, 5 + ei:6 + ei], None,
            ALU.is_equal).then_inc(SM["VE"], 1))
        inc("VE")

    x_ready_ve = [0] * NT
    xlnT_ve = [0] * NT
    xln_free_pe = [0]

    def emit_layernorm(t, first):
        xt = Xsb[:, t * 4096:(t + 1) * 4096]
        su, mu, ms_, vr, sd, rs, sq = (colz[:, 0:1], colz[:, 1:2], colz[:, 2:3],
                                       colz[:, 3:4], colz[:, 4:5], colz[:, 5:6],
                                       colz[:, 6:7])
        if first:
            W("v", "sxl", 16 * NT)
            W("s", "sxl", 16 * NT)
        else:
            W("v", "VE", x_ready_ve[t])
            W("s", "VE", x_ready_ve[t])
        E("v", lambda e, xt=xt: e.reduce_sum(su, xt, axis=AX)
          .then_inc(SM["VE"], 1))
        inc("VE")
        for sl in range(8):
            E("s", lambda e, xt=xt, sl=sl: e.activation(
                junkb[:, :], xt[:, sl * 512:(sl + 1) * 512], AF.Square,
                accum_out=colz[:, 8 + sl:9 + sl])
              .then_inc(SM["SE"], 1) if sl == 7 else
              e.activation(junkb[:, :], xt[:, sl * 512:(sl + 1) * 512],
                           AF.Square, accum_out=colz[:, 8 + sl:9 + sl]))
        inc("SE")
        W("v", "SE", cv("SE"))
        E("v", lambda e: e.reduce_sum(sq, colz[:, 8:16], axis=AX)
          .then_inc(SM["VE"], 1))
        W("v", "VE", inc("VE"))
        E("v", lambda e: e.tensor_scalar(mu, su, cims[:, 0:1], None, ALU.mult))
        E("v", lambda e: e.tensor_scalar(ms_, sq, cims[:, 0:1], None, ALU.mult))
        E("v", lambda e: e.scalar_tensor_tensor(vr, mu, mu[:, 0:1], ms_,
                                                ALU.mult, ALU.subtract)
          .then_inc(SM["VE"], 1))
        inc("VE")
        W("s", "VE", cv("VE"))
        E("s", lambda e: e.activation(sd, vr, AF.Sqrt, bias=f32(1e-5),
                                      scale=f32(-1.0)).then_inc(SM["SE"], 1))
        inc("SE")
        W("v", "SE", cv("SE"))
        W("v", "PE", xln_free_pe[0])
        E("v", lambda e: e.reciprocal(rs, sd))
        E("v", lambda e, xt=xt: e.tensor_scalar(xln_bf[:, :], xt, mu[:, 0:1],
                                                rs[:, 0:1], ALU.subtract,
                                                ALU.mult).then_inc(SM["VE"], 1))
        ve_ln = inc("VE")
        W("t", "VE", ve_ln)
        for gi in range(4):
            bank, bn = (psT, "psT") if gi % 2 == 0 else (psU, "psU")
            twb(bn)
            for kb in range(8):
                col = gi * 8 + kb
                mm(bank[:, kb * 128:(kb + 1) * 128],
                   xln_bf[:, col * 128:(col + 1) * 128], id_sb[:, :],
                   True, True, tr=True, pe=(kb == 7))
            W("v", "PE", cv("PE"))
            dst = xlnT[:, t * 4096 + gi * 1024: t * 4096 + (gi + 1) * 1024]
            E("v", lambda e, d=dst, b=bank: e.tensor_copy(d, b[:, :])
              .then_inc(SM["VE"], 1))
            bank_free[bn] = ("VE", inc("VE"))
        xln_free_pe[0] = cv("PE")
        xlnT_ve[t] = cv("VE")

    stg_cnt = {"q": [0, 0], "k": [0, 0], "v": [0, 0]}

    def emit_qkv(L):
        robuf_free = [0, 0]
        W("y", "CC", cv("CC"))
        for n in ("szs0", "szs1", "szl0", "szl1"):
            W("v", n, cv(n))
        for s in [8, 9, 10, 11, 0, 1, 2, 3, 4, 5, 6, 7]:
            if s == 0:
                emit_ag()
            load_wstrip(qkvw[L, s, :, 0:16, :], 0)
            load_wstrip(qkvw[L, s, :, 16:32, :], 1)
            W("t", "sw0", cv("sw0"))
            W("t", "sw1", cv("sw1"))
            pe_last = 0
            for t in range(NT):
                bank, bn = (psA, "psA") if t % 2 == 0 else (psB, "psB")
                twb(bn)
                W("t", "VE", xlnT_ve[t])
                for kt in range(32):
                    mm(bank[:, :],
                       xlnT[:, t * 4096 + kt * 128: t * 4096 + (kt + 1) * 128],
                       wb[kt // 16][:, (kt % 16) * 512:((kt % 16) + 1) * 512],
                       start=(kt == 0), stop=(kt == 31), pe=(kt == 31))
                pe_last = cv("PE")
                if s < 10:
                    W("v", "PE", pe_last)
                    rp = t % 2
                    W("v", "PE", robuf_free[rp])
                    ct = cos_sb[:, t * 64:(t + 1) * 64]
                    st_ = sin_sb[:, t * 64:(t + 1) * 64]
                    for i in range(4):
                        b0 = i * 128
                        E("v", lambda e, b0=b0, bank=bank, ct=ct:
                          e.tensor_mul(t1f[:, 0:64], bank[:, b0:b0 + 64], ct))
                        E("v", lambda e, b0=b0, bank=bank, ct=ct:
                          e.tensor_mul(t1f[:, 64:128],
                                       bank[:, b0 + 64:b0 + 128], ct))
                        E("v", lambda e, b0=b0, bank=bank, st_=st_:
                          e.tensor_mul(t2f[:, 0:64],
                                       bank[:, b0 + 64:b0 + 128], st_))
                        E("v", lambda e, b0=b0, bank=bank, st_=st_:
                          e.tensor_mul(t2f[:, 64:128], bank[:, b0:b0 + 64], st_))
                        E("v", lambda e, b0=b0, rp=rp:
                          e.tensor_sub(robuf[rp][:, b0:b0 + 64],
                                       t1f[:, 0:64], t2f[:, 0:64]))
                        if i == 3:
                            E("v", lambda e, b0=b0, rp=rp:
                              e.tensor_add(robuf[rp][:, b0 + 64:b0 + 128],
                                           t1f[:, 64:128], t2f[:, 64:128])
                              .then_inc(SM["VE"], 1))
                        else:
                            E("v", lambda e, b0=b0, rp=rp:
                              e.tensor_add(robuf[rp][:, b0 + 64:b0 + 128],
                                           t1f[:, 64:128], t2f[:, 64:128]))
                    bank_free[bn] = ("VE", inc("VE"))
                    W("t", "VE", cv("VE"))
                    bank2, bn2 = (psT, "psT") if t % 2 == 0 else (psU, "psU")
                    twb(bn2)
                    for i in range(4):
                        mm(bank2[:, i * 128:(i + 1) * 128],
                           robuf[rp][:, i * 128:(i + 1) * 128], id_sb[:, :],
                           True, True, tr=True, pe=(i == 3))
                    robuf_free[rp] = cv("PE")
                    kind = "q" if s < 8 else "k"
                    stg = qstage if kind == "q" else kstage
                    sn = ("sqs%d" if kind == "q" else "sks%d")
                    c = stg_cnt[kind]
                    p = (c[0] + c[1]) % 2
                    W("v", sn % p, 16 * c[p])
                    W("v", "PE", cv("PE"))
                    E("v", lambda e, p=p, b=bank2, stg=stg:
                      e.tensor_copy(stg[p][:, :], b[:, 0:512])
                      .then_inc(SM["VE"], 1))
                    bank_free[bn2] = ("VE", inc("VE"))
                    W("y", "VE", cv("VE"))
                    if kind == "q":
                        dst = qtp[t, 4 * s:4 * s + 4].transpose([1, 0, 2])
                    else:
                        g0 = 4 * (s - 8)
                        dst = agi_kt[g0:g0 + 4, :, t, :].transpose([1, 0, 2])
                    E("y", lambda e, d=dst, p=p, stg=stg, sm=SM[sn % p]:
                      e.dma_start(out=d, in_=stg[p][:, :]).then_inc(sm, 16))
                    c[p] += 1
                    inc(sn % p, 16)
                else:
                    c = stg_cnt["v"]
                    p = (c[0] + c[1]) % 2
                    W("v", "svs%d" % p, 16 * c[p])
                    W("v", "PE", pe_last)
                    E("v", lambda e, p=p, bank=bank:
                      e.tensor_copy(vstage[p][:, :], bank[:, :])
                      .then_inc(SM["VE"], 1))
                    bank_free[bn] = ("VE", inc("VE"))
                    W("y", "VE", cv("VE"))
                    dst = agi_v[t, :, (s - 10) * 512:(s - 10) * 512 + 512]
                    E("y", lambda e, d=dst, p=p, sm=SM["svs%d" % p]:
                      e.dma_start(out=d, in_=vstage[p][:, :]).then_inc(sm, 16))
                    c[p] += 1
                    inc("svs%d" % p, 16)
            wb_free_pe[0] = pe_last
            wb_free_pe[1] = pe_last

    def emit_ag():
        for n in ["sks0", "sks1", "svs0", "svs1", "skv0", "skv1"]:
            W("g", n, cv(n))
        E("g", lambda e: e.collective_compute(
            "AllGather", ALU.bypass,
            replica_groups=[[0, 1, 2, 3], [4, 5, 6, 7]],
            ins=[ag_in.opt()], outs=[ag_out_t.ap().opt()])
          .then_inc(SM["CC"], 1))
        inc("CC")

    last_z_ve = [0, 0, 0, 0]
    kv_free = [0, 0]
    qt_free = [0, 0]

    def emit_attention(L, grp):
        W("y", "sqs0", cv("sqs0"))
        W("y", "sqs1", cv("sqs1"))
        for t in range(4 * grp, 4 * grp + 4):
            tm = t % 4
            nch = t + 1
            for g in range(8):
                gp = g % 2
                W("y", "CC", cv("CC"))
                W("y", "PE", kv_free[gp])
                for kc in range(nch):
                    src = ago_kt[:, g, :, kc, :].transpose([1, 0, 2])
                    E("y", lambda e, s_=src, gp=gp, kc=kc, sm=SM["skv%d" % gp]:
                      e.dma_start(out=ktg[gp][:, kc * 512:(kc + 1) * 512], in_=s_)
                      .then_inc(sm, 16))
                    inc("skv%d" % gp, 16)
                    srcv = ago_v[:, kc, :, g * 128:(g + 1) * 128].transpose([1, 0, 2])
                    E("y", lambda e, s_=srcv, gp=gp, kc=kc, sm=SM["skv%d" % gp]:
                      e.dma_start(out=vg[gp][:, kc * 512:(kc + 1) * 512], in_=s_)
                      .then_inc(sm, 16))
                    inc("skv%d" % gp, 16)
                W("y", "PE", qt_free[gp])
                E("y", lambda e, t=t, g=g, gp=gp, sm=SM["sqt%d" % gp]:
                  e.dma_start(out=qtq[gp][:, :],
                              in_=qtp[t, 4 * g:4 * g + 4].transpose([1, 0, 2]))
                  .then_inc(sm, 16))
                inc("sqt%d" % gp, 16)

                se_exp = [[0] * nch for _ in range(4)]

                def emit_scores(i):
                    if i == 0:
                        W("t", "skv%d" % gp, cv("skv%d" % gp))
                        W("t", "sqt%d" % gp, cv("sqt%d" % gp))
                    W("s", "VE", last_z_ve[i])
                    for kc in range(nch):
                        bn = "psS%d" % (kc % 2)
                        bank = psS[kc % 2]
                        twb(bn)
                        diag = (kc == t)
                        mm(bank[:, :], qtq[gp][:, i * 128:(i + 1) * 128],
                           ktg[gp][:, kc * 512:(kc + 1) * 512],
                           start=True, stop=not diag, pe=not diag)
                        if diag:
                            mm(bank[:, :], atri_sb[:, :], mmask_sb[:, :],
                               start=False, stop=True, pe=True)
                        W("s", "PE", cv("PE"))
                        E("s", lambda e, i=i, kc=kc, bank=bank:
                          e.activation(Pst[i % 2][:, kc * 512:(kc + 1) * 512],
                                       bank[:, :], AF.Exp, bias=EBIAS,
                                       scale=SCALE,
                                       accum_out=zc[:, i * 8 + kc:i * 8 + kc + 1])
                          .then_inc(SM["SE"], 1))
                        se_exp[i][kc] = inc("SE")
                        bank_free[bn] = ("SE", se_exp[i][kc])
                    W("v", "SE", se_exp[i][nch - 1])
                    E("v", lambda e, i=i, nch=nch: e.reduce_sum(
                        colz[:, 8:9], zc[:, i * 8:i * 8 + nch], axis=AX))
                    E("v", lambda e: e.tensor_copy(colz[:, 2:3], colz[:, 2:3]))
                    E("v", lambda e: e.tensor_copy(colz[:, 2:3], colz[:, 2:3]))
                    E("v", lambda e, i=i: e.reciprocal(rinvq[:, i:i + 1],
                                                       colz[:, 8:9])
                      .then_inc(SM["VE"], 1))
                    last_z_ve[i] = inc("VE")

                def emit_ptav(i, first_av, last_av):
                    nblk = 4 * nch
                    for kb0 in range(0, nblk, 8):
                        kbn = min(8, nblk - kb0)
                        bank2, bn2 = ((psT, "psT") if (kb0 // 8) % 2 == 0
                                      else (psU, "psU"))
                        twb(bn2)
                        W("t", "SE", se_exp[i][min(nch - 1, (kb0 + kbn - 1) // 4)])
                        for kk in range(kbn):
                            kb = kb0 + kk
                            mm(bank2[:, kk * 128:(kk + 1) * 128],
                               Pst[i % 2][:, kb * 128:(kb + 1) * 128],
                               id_sb[:, :], True, True, tr=True,
                               pe=(kk == kbn - 1))
                        W("v", "PE", cv("PE"))
                        E("v", lambda e, kb0=kb0, kbn=kbn, b=bank2:
                          e.tensor_copy(pt_sb[:, kb0 * 128:(kb0 + kbn) * 128],
                                        b[:, 0:kbn * 128]).then_inc(SM["VE"], 1))
                        bank_free[bn2] = ("VE", inc("VE"))
                    W("t", "VE", cv("VE"))
                    if first_av:
                        eng, c = bank_free["psO"]
                        W("t", eng, c)
                    for kb in range(nblk):
                        mm(psO[:, i * 128:(i + 1) * 128],
                           pt_sb[:, kb * 128:(kb + 1) * 128],
                           vg[gp][:, kb * 128:(kb + 1) * 128],
                           start=(first_av and kb == 0),
                           stop=(last_av and kb == nblk - 1),
                           pe=(kb == nblk - 1))

                emit_scores(0)
                for i in range(1, 4):
                    emit_scores(i)
                    emit_ptav(i - 1, first_av=(i == 1), last_av=False)
                emit_ptav(3, first_av=False, last_av=True)
                qt_free[gp] = cv("PE")
                kv_free[gp] = cv("PE")
                W("v", "PE", cv("PE"))
                for i in range(4):
                    last = (i == 3)
                    if last:
                        E("v", lambda e, i=i: e.tensor_scalar(
                            ostage[:, i * 128:(i + 1) * 128],
                            psO[:, i * 128:(i + 1) * 128],
                            rinvq[:, i:i + 1], None, ALU.mult)
                          .then_inc(SM["VE"], 1))
                    else:
                        E("v", lambda e, i=i: e.tensor_scalar(
                            ostage[:, i * 128:(i + 1) * 128],
                            psO[:, i * 128:(i + 1) * 128],
                            rinvq[:, i:i + 1], None, ALU.mult))
                bank_free["psO"] = ("VE", inc("VE"))
                W("t", "VE", cv("VE"))
                twb("psT")
                for i in range(4):
                    mm(psT[:, i * 128:(i + 1) * 128],
                       ostage[:, i * 128:(i + 1) * 128], id_sb[:, :],
                       True, True, tr=True, pe=(i == 3))
                W("v", "PE", cv("PE"))
                E("v", lambda e, tm=tm, g=g: e.tensor_copy(
                    ot4[:, tm * 4096 + g * 512: tm * 4096 + (g + 1) * 512],
                    psT[:, 0:512]).then_inc(SM["VE"], 1))
                bank_free["psT"] = ("VE", inc("VE"))

    def emit_wo(L, grp):
        for s in range(8):
            load_wstrip(wow[L, s, :, 0:16, :], 0)
            load_wstrip(wow[L, s, :, 16:32, :], 1)
            W("t", "sw0", cv("sw0"))
            W("t", "sw1", cv("sw1"))
            W("t", "VE", cv("VE"))
            pe_last = 0
            for t in range(4 * grp, 4 * grp + 4):
                tm = t % 4
                bank, bn = (psA, "psA") if t % 2 == 0 else (psB, "psB")
                twb(bn)
                for kb in range(32):
                    mm(bank[:, :],
                       ot4[:, tm * 4096 + kb * 128: tm * 4096 + (kb + 1) * 128],
                       wb[kb // 16][:, (kb % 16) * 512:((kb % 16) + 1) * 512],
                       start=(kb == 0), stop=(kb == 31), pe=(kb == 31))
                pe_last = cv("PE")
                W("v", "PE", pe_last)
                xs = Xsb[:, t * 4096 + s * 512: t * 4096 + (s + 1) * 512]
                E("v", lambda e, xs=xs, bank=bank:
                  e.tensor_add(xs, xs, bank[:, :]).then_inc(SM["VE"], 1))
                bank_free[bn] = ("VE", inc("VE"))
                if s == 7:
                    x_ready_ve[t] = cv("VE")
            wb_free_pe[0] = pe_last
            wb_free_pe[1] = pe_last

    def emit_ut(t, scale_pcol):
        twb("psX")
        W("t", "VE", xlnT_ve[t])
        W("t", "scl2", cv("scl2"))
        for kt in range(32):
            mm(psX[:, 0:128],
               xlnT[:, t * 4096 + kt * 128: t * 4096 + (kt + 1) * 128],
               acat_sb[:, kt * 128:(kt + 1) * 128],
               start=(kt == 0), stop=(kt == 31), pe=(kt == 31))
        W("v", "PE", cv("PE"))
        E("v", lambda e: e.tensor_copy(ustage[:, :], psX[:, 0:128])
          .then_inc(SM["VE"], 1))
        bank_free["psX"] = ("VE", inc("VE"))
        W("t", "VE", cv("VE"))
        twb("psT")
        mm(psT[:, 0:128], ustage[:, :], id_sb[:, :], True, True,
           tr=True, pe=True)
        W("v", "PE", cv("PE"))
        if scale_pcol:
            E("v", lambda e, t=t: e.tensor_scalar(
                ut_sb[:, t * 128:(t + 1) * 128], psT[:, 0:128],
                pcol_sb[:, 0:1], None, ALU.mult).then_inc(SM["VE"], 1))
        else:
            E("v", lambda e: e.tensor_copy(htmp[:, :], psT[:, 0:128])
              .then_inc(SM["VE"], 1))
        bank_free["psT"] = ("VE", inc("VE"))

    def emit_ffn0():
        for t in range(NT):
            emit_layernorm(t, first=False)
            emit_ut(t, scale_pcol=True)
        zs_cnt = [0, 0]
        W("s", "svs0", cv("svs0"))
        W("s", "svs1", cv("svs1"))
        W("y", "sks0", cv("sks0"))
        W("y", "sks1", cv("sks1"))
        for s in range(8):
            load_wstrip(ffnw[0, s, :, 0:16, :], 0)
            load_wstrip(ffnw[0, s, :, 16:32, :], 1)
            load_bias_slice(0, s)
            W("t", "sw0", cv("sw0"))
            W("t", "sw1", cv("sw1"))
            W("t", "scl2", cv("scl2"))
            pe_last = 0
            for t in range(NT):
                bank, bn = (psA, "psA") if t % 2 == 0 else (psB, "psB")
                twb(bn)
                for kt in range(32):
                    mm(bank[:, :],
                       xlnT[:, t * 4096 + kt * 128: t * 4096 + (kt + 1) * 128],
                       wb[kt // 16][:, (kt % 16) * 512:((kt % 16) + 1) * 512],
                       start=(kt == 0), stop=False)
                mm(bank[:, :], ones_sb[:, :], bias_sb[:, :],
                   start=False, stop=True, pe=True)
                pe_last = cv("PE")
                bias_free_pe[0] = pe_last
                p = (zs_cnt[0] + zs_cnt[1]) % 2
                W("s", "szs%d" % p, 16 * zs_cnt[p])
                W("s", "PE", pe_last)
                E("s", lambda e, p=p, bank=bank:
                  e.copy(zstage[p][:, :], bank[:, :]).then_inc(SM["SE"], 1))
                se_z = inc("SE")
                E("s", lambda e, t=t, s=s, bank=bank: e.activation(
                    junkb[:, :], bank[:, :], AF.Square,
                    accum_out=szc[:, t * 8 + s: t * 8 + s + 1])
                  .then_inc(SM["SE"], 1))
                bank_free[bn] = ("SE", inc("SE"))
                W("y", "SE", se_z)
                E("y", lambda e, t=t, s=s, p=p, sm=SM["szs%d" % p]:
                  e.dma_start(out=zpark[t, :, s * 512:(s + 1) * 512],
                              in_=zstage[p][:, :]).then_inc(sm, 16))
                zs_cnt[p] += 1
                inc("szs%d" % p, 16)
            wb_free_pe[0] = pe_last
            wb_free_pe[1] = pe_last
        W("t", "VE", cv("VE"))
        for t in range(NT):
            for s in range(8):
                bank, bn = (psA, "psA") if s % 2 == 0 else (psB, "psB")
                twb(bn)
                mm(bank[:, :], ut_sb[:, t * 128:(t + 1) * 128],
                   bcat_sb[:, s * 512:(s + 1) * 512], True, True, pe=True)
                W("s", "PE", cv("PE"))
                E("s", lambda e, t=t, s=s, bank=bank: e.activation(
                    junkb[:, :], bank[:, :], AF.Square,
                    accum_out=smc[:, t * 8 + s: t * 8 + s + 1])
                  .then_inc(SM["SE"], 1))
                bank_free[bn] = ("SE", inc("SE"))
        W("v", "SE", cv("SE"))
        for t in range(NT):
            E("v", lambda e, t=t: e.reduce_sum(zc[:, t:t + 1],
                                               szc[:, t * 8:t * 8 + 8], axis=AX))
        for t in range(NT):
            E("v", lambda e, t=t: e.reduce_sum(zc[:, 8 + t:9 + t],
                                               smc[:, t * 8:t * 8 + 8], axis=AX))
        E("v", lambda e: e.tensor_copy(colz[:, 2:3], colz[:, 2:3])
          .then_inc(SM["VE"], 1))
        inc("VE")
        W("s", "VE", cv("VE"))
        E("s", lambda e: e.activation(zc[:, 16:24], zc[:, 0:8], AF.Sqrt))
        E("s", lambda e: e.activation(zc[:, 24:32], zc[:, 8:16], AF.Sqrt)
          .then_inc(SM["SE"], 1))
        inc("SE")
        W("v", "SE", cv("SE"))
        E("v", lambda e: e.tensor_scalar(szc[:, 0:8], zc[:, 24:32],
                                         cims[:, 3:4], None, ALU.add))
        E("v", lambda e: e.tensor_copy(colz[:, 2:3], colz[:, 2:3]))
        E("v", lambda e: e.tensor_copy(colz[:, 2:3], colz[:, 2:3]))
        E("v", lambda e: e.reciprocal(szc[:, 8:16], szc[:, 0:8]))
        E("v", lambda e: e.tensor_copy(colz[:, 2:3], colz[:, 2:3]))
        E("v", lambda e: e.tensor_copy(colz[:, 2:3], colz[:, 2:3]))
        E("v", lambda e: e.tensor_mul(szc[:, 16:24], zc[:, 16:24],
                                      szc[:, 8:16]))
        E("v", lambda e: e.tensor_copy(colz[:, 2:3], colz[:, 2:3]))
        E("v", lambda e: e.tensor_copy(colz[:, 2:3], colz[:, 2:3]))
        E("v", lambda e: e.tensor_scalar(gam[:, 0:8], szc[:, 16:24],
                                         cims[:, 1:2], cims[:, 2:3],
                                         ALU.mult, ALU.min))
        E("v", lambda e: e.tensor_copy(colz[:, 2:3], colz[:, 2:3]))
        E("v", lambda e: e.tensor_copy(colz[:, 2:3], colz[:, 2:3])
          .then_inc(SM["VE"], 1))
        inc("VE")
        ve_gam = cv("VE")
        W("y", "szs0", cv("szs0"))
        W("y", "szs1", cv("szs1"))
        zl_cnt = [0, 0]
        zld_free = [ve_gam, ve_gam]
        for t in range(NT):
            for s in range(8):
                p = (zl_cnt[0] + zl_cnt[1]) % 2
                W("y", "VE", zld_free[p])
                E("y", lambda e, t=t, s=s, p=p, sm=SM["szl%d" % p]:
                  e.dma_start(out=zld[p][:, :],
                              in_=zpark[t, :, s * 512:(s + 1) * 512])
                  .then_inc(sm, 16))
                zl_cnt[p] += 1
                inc("szl%d" % p, 16)
                bank, bn = (psA, "psA") if s % 2 == 0 else (psB, "psB")
                twb(bn)
                mm(bank[:, :], ut_sb[:, t * 128:(t + 1) * 128],
                   bcat_sb[:, s * 512:(s + 1) * 512], True, True, pe=True)
                W("v", "PE", cv("PE"))
                W("v", "VE", ve_gam)
                W("v", "szl%d" % p, cv("szl%d" % p))
                xs = Xsb[:, t * 4096 + s * 512: t * 4096 + (s + 1) * 512]
                E("v", lambda e, xs=xs, p=p: e.tensor_add(xs, xs, zld[p][:, :]))
                E("v", lambda e, t=t, xs=xs, bank=bank:
                  e.scalar_tensor_tensor(xs, bank[:, :], gam[:, t:t + 1],
                                         xs, ALU.mult, ALU.add)
                  .then_inc(SM["VE"], 1))
                bank_free[bn] = ("VE", inc("VE"))
                zld_free[p] = cv("VE")
                if s == 7:
                    x_ready_ve[t] = cv("VE")

    def emit_ffn1():
        pr_free_ve = 0
        for t in range(NT):
            emit_layernorm(t, first=False)
            emit_ut(t, scale_pcol=False)
            twb("psX")
            for kt in range(32):
                mm(psX[:, 0:15],
                   xlnT[:, t * 4096 + kt * 128: t * 4096 + (kt + 1) * 128],
                   rwt_sb[:, kt * 15:(kt + 1) * 15],
                   start=(kt == 0), stop=False)
            mm(psX[:, 0:15], ones_sb[:, :], rb_sb[:, :], start=False,
               stop=True, pe=True)
            W("s", "PE", cv("PE"))
            W("s", "VE", pr_free_ve)
            E("s", lambda e: e.activation(pr_sb[:, 0:15], psX[:, 0:15], AF.Exp,
                                          accum_out=rcol[:, 0:1])
              .then_inc(SM["SE"], 1))
            bank_free["psX"] = ("SE", inc("SE"))
            W("v", "SE", cv("SE"))
            E("v", lambda e: e.reciprocal(rcol[:, 1:2], rcol[:, 0:1]))
            E("v", lambda e: e.memset(rnk[:, 16:17], 0.0))
            E("v", lambda e: e.tensor_copy(colz[:, 2:3], colz[:, 2:3]))
            E("v", lambda e: e.tensor_scalar(pr_sb[:, 0:15], pr_sb[:, 0:15],
                                             rcol[:, 1:2], None, ALU.mult))
            E("v", lambda e: e.tensor_copy(colz[:, 2:3], colz[:, 2:3]))
            E("v", lambda e: e.tensor_copy(colz[:, 2:3], colz[:, 2:3]))
            for n in range(15):
                E("v", lambda e, n=n: e.tensor_scalar(
                    junk15[:, 0:15], pr_sb[:, 0:15], pr_sb[:, n:n + 1], None,
                    ALU.is_gt, ALU.add, accum_out=rnk[:, n:n + 1]))
            for n in range(1, 15):
                E("v", lambda e, n=n: e.tensor_scalar(
                    junk15[:, 0:n], pr_sb[:, 0:n], pr_sb[:, n:n + 1], None,
                    ALU.is_equal, ALU.add, accum_out=rnk[:, 16 + n:17 + n]))
            E("v", lambda e: e.tensor_copy(colz[:, 2:3], colz[:, 2:3]))
            E("v", lambda e: e.tensor_copy(colz[:, 2:3], colz[:, 2:3]))
            E("v", lambda e: e.tensor_add(rnk[:, 0:15], rnk[:, 0:15],
                                          rnk[:, 16:31]))
            E("v", lambda e: e.tensor_copy(colz[:, 2:3], colz[:, 2:3]))
            E("v", lambda e: e.tensor_copy(colz[:, 2:3], colz[:, 2:3]))
            E("v", lambda e: e.tensor_scalar(rnk[:, 16:31], rnk[:, 0:15],
                                             cims[:, 4:5], None, ALU.is_lt))
            E("v", lambda e: e.tensor_copy(colz[:, 2:3], colz[:, 2:3]))
            E("v", lambda e: e.tensor_copy(colz[:, 2:3], colz[:, 2:3]))
            E("v", lambda e: e.tensor_mul(psel[:, 0:15], pr_sb[:, 0:15],
                                          rnk[:, 16:31]))
            E("v", lambda e: e.memset(psel[:, 15:16], 0.0)
              .then_inc(SM["VE"], 1))
            pr_free_ve = inc("VE")
            W("t", "VE", cv("VE"))
            twb("psU")
            mm(psU[0:16, 0:128], psel[:, 0:16], id_sb[:, :], True, True,
               tr=True, pe=True)
            W("v", "PE", cv("PE"))
            E("v", lambda e: e.tensor_copy(pselT[:, :], psU[0:15, 0:128])
              .then_inc(SM["VE"], 1))
            bank_free["psU"] = ("VE", inc("VE"))
            W("t", "VE", cv("VE"))
            twb("psX")
            mm(psX[0:8, 0:128], msb[:, :], pselT[:, :], True, True, pe=True)
            W("v", "PE", cv("PE"))
            E("v", lambda e: e.tensor_copy(wt_bf[:, :], psX[0:8, 0:128])
              .then_inc(SM["VE"], 1))
            bank_free["psX"] = ("VE", inc("VE"))
            W("t", "VE", cv("VE"))
            twb("psX")
            mm(psX[:, 128:256], esb[:, :], wt_bf[:, :], True, True, pe=True)
            W("v", "PE", cv("PE"))
            E("v", lambda e: e.tensor_copy(wf_bf[:, :], psX[:, 128:256])
              .then_inc(SM["VE"], 1))
            bank_free["psX"] = ("VE", inc("VE"))
            E("v", lambda e: e.tensor_copy(colz[:, 2:3], colz[:, 2:3]))
            E("v", lambda e: e.tensor_copy(colz[:, 2:3], colz[:, 2:3]))
            E("v", lambda e, t=t: e.tensor_mul(ut_sb[:, t * 128:(t + 1) * 128],
                                               htmp[:, :], wf_bf[:, :])
              .then_inc(SM["VE"], 1))
            inc("VE")
        for s in range(8):
            load_wstrip(ffnw[1, s, :, 0:16, :], 0)
            load_wstrip(ffnw[1, s, :, 16:32, :], 1)
            load_bias_slice(1, s)
            W("t", "sw0", cv("sw0"))
            W("t", "sw1", cv("sw1"))
            W("t", "scl2", cv("scl2"))
            W("t", "VE", cv("VE"))
            pe_last = 0
            for t in range(NT):
                bank, bn = (psA, "psA") if t % 2 == 0 else (psB, "psB")
                twb(bn)
                for kt in range(32):
                    mm(bank[:, :],
                       xlnT[:, t * 4096 + kt * 128: t * 4096 + (kt + 1) * 128],
                       wb[kt // 16][:, (kt % 16) * 512:((kt % 16) + 1) * 512],
                       start=(kt == 0), stop=False)
                mm(bank[:, :], ones_sb[:, :], bias_sb[:, :],
                   start=False, stop=False)
                mm(bank[:, :], ut_sb[:, t * 128:(t + 1) * 128],
                   bcat_sb[:, s * 512:(s + 1) * 512], start=False, stop=True,
                   pe=True)
                pe_last = cv("PE")
                bias_free_pe[0] = pe_last
                W("v", "PE", pe_last)
                xs = Xsb[:, t * 4096 + s * 512: t * 4096 + (s + 1) * 512]
                E("v", lambda e, xs=xs, bank=bank:
                  e.tensor_add(xs, xs, bank[:, :]).then_inc(SM["VE"], 1))
                bank_free[bn] = ("VE", inc("VE"))
                if s == 7:
                    x_ready_ve[t] = cv("VE")
                    W("g", "VE", cv("VE"))
                    E("g", lambda e, t=t: e.dma_start(
                        out=y[t * 128:(t + 1) * 128, :],
                        in_=Xsb[:, t * 4096:(t + 1) * 4096])
                      .then_inc(SM["sys"], 16))
                    inc("sys", 16)
            wb_free_pe[0] = pe_last
            wb_free_pe[1] = pe_last

    def load_layer_consts(L):
        W("g", "PE", cv("PE"))
        W("g", "SE", cv("SE"))
        pairs = [(acat_sb, acat[L].rearrange("p k e -> p (k e)")),
                 (bcat_sb, bcat[L])]
        if L == 1:
            pairs.append((rwt_sb, rwt.rearrange("p k n -> p (k n)")))
        for dst, src in pairs:
            E("g", lambda e, d=dst, s=src: e.dma_start(out=d[:, :], in_=s)
              .then_inc(SM["scl2"], 16))
            inc("scl2", 16)

    for L in range(2):
        phase_barrier()
        for t in range(NT):
            emit_layernorm(t, first=(L == 0))
        W("t", "scl", cv("scl"))
        emit_qkv(L)
        phase_barrier()
        for grp in range(2):
            emit_attention(L, grp)
            emit_wo(L, grp)
        load_layer_consts(L)
        phase_barrier()
        if L == 0:
            emit_ffn0()
        else:
            emit_ffn1()
        phase_barrier()
    W("g", "sys", cv("sys"))

    with nc.Block() as blk:
        @blk.tensor
        def _(e):
            for f in prog["t"]:
                f(e)

        @blk.vector
        def _(e):
            for f in prog["v"]:
                f(e)

        @blk.scalar
        def _(e):
            for f in prog["s"]:
                f(e)

        @blk.gpsimd
        def _(e):
            for f in prog["g"]:
                f(e)

        @blk.sync
        def _(e):
            for f in prog["y"]:
                f(e)

    return nc


def _tile_w(w):
    Din, Dout = w.shape
    ns, kt = Dout // 512, Din // 128
    return np.ascontiguousarray(w.reshape(kt, 128, ns, 512).transpose(2, 1, 0, 3))


_NC_CACHE = [None]


def kernel(L0_attn_x, L0_attn_Wq, L0_attn_Wk, L0_attn_Wv, L0_attn_Wo,
           L0_ffn_W, L0_ffn_b, L0_ffn_A, L0_ffn_B, L0_ffn_p_scores,
           L1_attn_w_qkv, L1_attn_w_o, L1_ffn_base_W, L1_ffn_base_b,
           L1_ffn_router_W, L1_ffn_router_b, L1_ffn_A, L1_ffn_B,
           L1_ffn_expert_map):
    x = np.asarray(L0_attn_x, dtype=np.float32)

    qkv0 = np.concatenate([np.asarray(L0_attn_Wq, np.float32),
                           np.asarray(L0_attn_Wk, np.float32),
                           np.asarray(L0_attn_Wv, np.float32)], axis=1)
    wq1 = np.array(L1_attn_w_qkv, dtype=np.float32)
    perm = np.arange(128).reshape(64, 2).T.reshape(-1)
    for h in range(32):
        wq1[:, h * 128:(h + 1) * 128] = wq1[:, h * 128 + perm]
    for g in range(8):
        b0 = 4096 + g * 128
        wq1[:, b0:b0 + 128] = wq1[:, b0 + perm]
    qkvw = np.stack([_tile_w(qkv0), _tile_w(wq1)])
    wow = np.stack([_tile_w(np.asarray(L0_attn_Wo, np.float32)),
                    _tile_w(np.asarray(L1_attn_w_o, np.float32))])
    ffnw = np.stack([
        _tile_w(np.ascontiguousarray(np.asarray(L0_ffn_W, np.float32).T)),
        _tile_w(np.ascontiguousarray(np.asarray(L1_ffn_base_W, np.float32).T))])
    ac0 = np.asarray(L0_ffn_A, np.float32).transpose(1, 0, 2).reshape(4096, 128)
    ac1 = np.asarray(L1_ffn_A, np.float32).transpose(2, 0, 1).reshape(4096, 128)
    acat = np.stack([np.ascontiguousarray(a.reshape(32, 128, 128)
                                          .transpose(1, 0, 2))
                     for a in (ac0, ac1)])
    bcat = np.stack([np.asarray(L0_ffn_B, np.float32).reshape(128, 4096),
                     np.ascontiguousarray(np.asarray(L1_ffn_B, np.float32)
                                          .transpose(0, 2, 1)
                                          .reshape(128, 4096))])
    rwt = np.ascontiguousarray(np.asarray(L1_ffn_router_W, np.float32).T
                               .reshape(32, 128, 15).transpose(1, 0, 2))
    biasv = np.stack([np.asarray(L0_ffn_b, np.float32).reshape(1, 4096),
                      np.asarray(L1_ffn_base_b, np.float32).reshape(1, 4096)])
    rbias = np.asarray(L1_ffn_router_b, np.float32).reshape(1, 15)
    pcolv = np.repeat(np.asarray(L0_ffn_p_scores, np.float32), 16).reshape(128, 1)
    emap = np.asarray(L1_ffn_expert_map, np.int32).reshape(15, 1)

    identc = np.eye(128, dtype=np.float32).astype(BF)
    dd = np.arange(128)
    atric = (dd[:, None] >= dd[None, :]).astype(np.float32).astype(BF)
    onesc = np.ones((1, 128), np.float32).astype(BF)
    esbc = (2.0 * (np.arange(128)[None, :] // 16 == np.arange(8)[:, None])
            ).astype(np.float32).astype(BF)
    inv = (1.0 / (np.float32(10000.0) **
                  (np.arange(0, 128, 2, dtype=np.float32) / np.float32(128.0))
                  )).astype(np.float32)
    cims_h = np.ascontiguousarray(np.tile(np.array(
        [1.0 / 4096.0, 0.5, 1.0, 1e-6, 7.5, 0, 1, 2, 3, 4, 5, 6, 7],
        np.float32), (128, 1)))

    in_maps = []
    shard_blocks = []
    for core in range(8):
        b, j = divmod(core, 4)
        blocks = np.arange(j, 32, 4)
        shard_blocks.append((b, blocks))
        xs = np.ascontiguousarray(
            x[b].reshape(32, 128, 4096)[blocks].reshape(1024, 4096))
        pos = (blocks[:, None] * 128 + np.arange(128)[None, :]).reshape(-1)
        ang = pos[:, None].astype(np.float32) * inv[None, :]
        c64 = np.cos(ang).astype(np.float32)
        s64 = np.sin(ang).astype(np.float32)
        cosc = np.ascontiguousarray(
            c64.reshape(8, 128, 64).transpose(1, 0, 2).reshape(128, 512))
        sincv = np.ascontiguousarray(
            s64.reshape(8, 128, 64).transpose(1, 0, 2).reshape(128, 512))
        kr = np.arange(512)[None, :]
        mmaskc = (np.float32(NEG) * (kr >= dd[:, None] + j * 128 + 1)
                  ).astype(np.float32).astype(BF)
        in_maps.append({
            "xsh": xs, "qkvw": qkvw, "wow": wow, "ffnw": ffnw, "acat": acat,
            "bcat": bcat, "rwt": rwt, "biasv": biasv, "rbias": rbias,
            "pcolv": pcolv, "emap": emap, "identc": identc, "atric": atric,
            "onesc": onesc, "esbc": esbc, "cosc": cosc, "sinc": sincv,
            "mmaskc": mmaskc, "cimsc": cims_h,
        })

    if _NC_CACHE[0] is None:
        _NC_CACHE[0] = build_nc()
    nc = _NC_CACHE[0]
    res = run_bass_kernel_spmd(nc, in_maps, core_ids=list(range(8)))

    out = np.empty((2, 4096, 4096), dtype=np.float32)
    ov = out.reshape(2, 32, 128, 4096)
    for core in range(8):
        b, blocks = shard_blocks[core]
        yc = np.asarray(res.results[core]["y"]).reshape(8, 128, 4096)
        ov[b, blocks] = yc
    return out


# version 12
# speedup vs baseline: 1.0611x; 1.0312x over previous
import numpy as np
import ml_dtypes
import concourse.bass as bass
import concourse.mybir as mybir
from concourse.bass_utils import run_bass_kernel_spmd

F32 = mybir.dt.float32
BF16 = mybir.dt.bfloat16
I32 = mybir.dt.int32
AF = mybir.ActivationFunctionType
ALU = mybir.AluOpType
AX = mybir.AxisListType.X
BF = ml_dtypes.bfloat16
f32 = np.float32

NT = 8
SCALE = f32(1.0 / np.sqrt(128.0))
EBIAS = f32(-15.0)
NEG = -1.0e30
AGZ = 1024 * 1024


def build_nc():
    nc = bass.Bass("TRN2", target_bir_lowering=False, debug=False)

    xsh = nc.dram_tensor("xsh", [1024, 4096], F32, kind="ExternalInput").ap()
    qkvw = nc.dram_tensor("qkvw", [2, 12, 128, 32, 512], F32, kind="ExternalInput").ap()
    wow = nc.dram_tensor("wow", [2, 8, 128, 32, 512], F32, kind="ExternalInput").ap()
    ffnw = nc.dram_tensor("ffnw", [2, 8, 128, 32, 512], F32, kind="ExternalInput").ap()
    acat = nc.dram_tensor("acat", [2, 128, 32, 128], F32, kind="ExternalInput").ap()
    bcat = nc.dram_tensor("bcat", [2, 128, 4096], F32, kind="ExternalInput").ap()
    rwt = nc.dram_tensor("rwt", [128, 32, 15], F32, kind="ExternalInput").ap()
    biasv = nc.dram_tensor("biasv", [2, 1, 4096], F32, kind="ExternalInput").ap()
    rbias = nc.dram_tensor("rbias", [1, 15], F32, kind="ExternalInput").ap()
    pcolv = nc.dram_tensor("pcolv", [128, 1], F32, kind="ExternalInput").ap()
    emap = nc.dram_tensor("emap", [15, 1], I32, kind="ExternalInput").ap()
    identc = nc.dram_tensor("identc", [128, 128], BF16, kind="ExternalInput").ap()
    atric = nc.dram_tensor("atric", [128, 128], BF16, kind="ExternalInput").ap()
    onesc = nc.dram_tensor("onesc", [1, 128], BF16, kind="ExternalInput").ap()
    esbc = nc.dram_tensor("esbc", [8, 128], BF16, kind="ExternalInput").ap()
    cosc = nc.dram_tensor("cosc", [128, 512], F32, kind="ExternalInput").ap()
    sinc = nc.dram_tensor("sinc", [128, 512], F32, kind="ExternalInput").ap()
    mmaskc = nc.dram_tensor("mmaskc", [128, 512], BF16, kind="ExternalInput").ap()
    cimsc = nc.dram_tensor("cimsc", [128, 13], F32, kind="ExternalInput").ap()
    y = nc.dram_tensor("y", [1024, 4096], F32, kind="ExternalOutput").ap()

    qtp = nc.dram_tensor("qtp", [8, 32, 128, 128], BF16, kind="Internal").ap()
    ag_in_t = nc.dram_tensor("ag_in", [2 * AGZ], BF16, kind="Internal")
    ag_out_t = nc.dram_tensor("ag_out", [8 * AGZ], BF16, kind="Internal")
    zpark = nc.dram_tensor("zpark", [8, 128, 4096], BF16, kind="Internal").ap()

    ag_in = ag_in_t.ap()
    agi_kt = ag_in[0:AGZ].rearrange("(g d t k) -> g d t k", g=8, d=128, t=8, k=128)
    agi_v = ag_in[AGZ:2 * AGZ].rearrange("(t k v) -> t k v", t=8, k=128, v=1024)
    ago5 = ag_out_t.ap().rearrange("(r s z) -> r s z", r=4, s=2, z=AGZ)
    ago_kt = ago5[:, 0, :].rearrange("r (g d t k) -> r g d t k", g=8, d=128, t=8, k=128)
    ago_v = ago5[:, 1, :].rearrange("r (t k v) -> r t k v", t=8, k=128, v=1024)

    A = nc.alloc_sbuf_tensor
    Xsb = A("Xsb", [128, 8 * 4096], BF16)
    cos_sb = A("cos_sb", [128, 512], F32)
    sin_sb = A("sin_sb", [128, 512], F32)
    id_sb = A("id_sb", [128, 128], BF16)
    atri_sb = A("atri_sb", [128, 128], BF16)
    mmask_sb = A("mmask_sb", [128, 512], BF16)
    ones_sb = A("ones_sb", [1, 128], BF16)
    esb = A("esb", [8, 128], BF16)
    pcol_sb = A("pcol_sb", [128, 1], F32)
    bias_sb = A("bias_sb", [1, 512], BF16)
    rb_sb = A("rb_sb", [1, 15], BF16)
    emap_i = A("emap_i", [15, 1], I32)
    emap_f = A("emap_f", [15, 1], F32)
    msb = A("msb", [15, 8], BF16)
    cims = A("cims", [128, 13], F32)
    wb = [A("wb0", [128, 16 * 512], BF16), A("wb1", [128, 16 * 512], BF16)]
    ut_sb = A("ut_sb", [128, 8 * 128], BF16)
    colz = A("colz", [128, 16], F32)
    szc = A("szc", [128, 64], F32)
    smc = A("smc", [128, 64], F32)
    gam = A("gam", [128, 8], F32)
    zc = A("zc", [128, 32], F32)
    rinvq = A("rinvq", [128, 4], F32)
    pr_sb = A("pr_sb", [128, 16], F32)
    psel = A("psel", [128, 16], BF16)
    junk15 = A("junk15", [128, 16], F32)
    rnk = A("rnk", [128, 32], F32)
    rcol = A("rcol", [128, 8], F32)
    pselT = A("pselT", [15, 128], BF16)
    wt_bf = A("wt_bf", [8, 128], BF16)
    junkb = A("junkb", [128, 512], BF16)
    t1f = A("t1f", [128, 128], F32)
    t2f = A("t2f", [128, 128], F32)
    robuf = [A("rob0", [128, 512], BF16), A("rob1", [128, 512], BF16)]
    qstage = [A("qst0", [128, 512], BF16), A("qst1", [128, 512], BF16)]
    kstage = [A("kst0", [128, 512], BF16), A("kst1", [128, 512], BF16)]
    vstage = [A("vst0", [128, 512], BF16), A("vst1", [128, 512], BF16)]
    zstage = vstage
    zld = kstage
    ostage = None
    qtq = [A("qtq0", [128, 512], BF16), A("qtq1", [128, 512], BF16)]
    ostage = robuf[0]
    htmp = robuf[1][:, 0:128]
    ustage = robuf[1][:, 128:256]
    wf_bf = robuf[1][:, 256:384]
    rwt_sb = A("rwt_sb", [128, 480], BF16)

    ARENA_COLS = 45056
    arena = A("arena", [128, ARENA_COLS], BF16)

    def av(off, cols):
        return arena[:, off:off + cols]

    xlnT = av(0, 32768)
    ot4 = av(0, 16384)
    ktg = [av(16384, 4096), av(20480, 4096)]
    vg = [av(24576, 4096), av(28672, 4096)]
    Pst = [av(32768, 4096), av(36864, 4096)]
    pt_sb = av(40960, 4096)
    acat_sb = av(32768, 4096)
    bcat_sb = av(36864, 4096)
    xln_bf = av(40960, 4096)

    psA = nc.alloc_psum_tensor("psA", [128, 512], F32)
    psB = nc.alloc_psum_tensor("psB", [128, 512], F32)
    psT = nc.alloc_psum_tensor("psT", [128, 1024], BF16)
    psU = nc.alloc_psum_tensor("psU", [128, 1024], BF16)
    psS = [nc.alloc_psum_tensor("psS0", [128, 512], F32),
           nc.alloc_psum_tensor("psS1", [128, 512], F32)]
    psO = nc.alloc_psum_tensor("psO", [128, 512], F32)
    psX = nc.alloc_psum_tensor("psX", [128, 512], F32)

    sem_names = ["PE", "VE", "SE", "CC",
                 "sxl", "scl", "scl2", "sw0", "sw1", "sw2", "skv0", "skv1",
                 "sqt0", "sqt1", "sqs0", "sqs1", "sks0", "sks1",
                 "svs0", "svs1", "szs0", "szs1", "szl0", "szl1", "sys"]
    SM = {n: nc.alloc_semaphore(n) for n in sem_names}
    CNT = {n: 0 for n in sem_names}
    prog = {k: [] for k in "tvsgy"}

    def E(eng, fn):
        prog[eng].append(fn)

    def inc(n, k=1):
        CNT[n] += k
        return CNT[n]

    def cv(n):
        return CNT[n]

    def W(eng, sem, val):
        if val <= 0:
            return
        E(eng, lambda e, s=SM[sem], v=val: e.wait_ge(s, v))

    def mm(out, lhsT, rhs, start, stop, tr=False, pe=False):
        if pe:
            if tr:
                E("t", lambda e, o=out, l=lhsT, r=rhs, a=start, b=stop:
                  e.matmul(o, l, r, start=a, stop=b, is_transpose=True)
                  .then_inc(SM["PE"], 1))
            else:
                E("t", lambda e, o=out, l=lhsT, r=rhs, a=start, b=stop:
                  e.matmul(o, l, r, start=a, stop=b).then_inc(SM["PE"], 1))
            return inc("PE")
        if tr:
            E("t", lambda e, o=out, l=lhsT, r=rhs, a=start, b=stop:
              e.matmul(o, l, r, start=a, stop=b, is_transpose=True))
        else:
            E("t", lambda e, o=out, l=lhsT, r=rhs, a=start, b=stop:
              e.matmul(o, l, r, start=a, stop=b))
        return None

    bank_free = {"psA": ("VE", 0), "psB": ("VE", 0), "psT": ("VE", 0),
                 "psU": ("VE", 0), "psS0": ("SE", 0), "psS1": ("SE", 0),
                 "psO": ("VE", 0), "psX": ("VE", 0)}

    def twb(name):
        eng, c = bank_free[name]
        W("t", eng, c)

    def phase_barrier():
        pe0, ve0, se0 = cv("PE"), cv("VE"), cv("SE")
        W("t", "VE", ve0); W("t", "SE", se0)
        W("v", "PE", pe0); W("v", "SE", se0)
        W("s", "PE", pe0); W("s", "VE", ve0)
        W("y", "PE", pe0); W("y", "VE", ve0); W("y", "SE", se0)
        W("g", "PE", pe0); W("g", "VE", ve0); W("g", "SE", se0)

    wb_free_pe = [0, 0]

    def load_wstrip(src_ap, buf):
        W("g", "PE", wb_free_pe[buf])
        E("g", lambda e, a=src_ap, b=wb[buf], s=SM["sw%d" % buf]:
          e.dma_start(out=b[:, :], in_=a).then_inc(s, 16))
        inc("sw%d" % buf, 16)

    bias_free_pe = [0]

    def load_bias_slice(L, s):
        W("g", "PE", bias_free_pe[0])
        E("g", lambda e, L=L, s=s: e.dma_start(
            out=bias_sb[:, :], in_=biasv[L, :, s * 512:(s + 1) * 512])
          .then_inc(SM["scl2"], 16))
        inc("scl2", 16)

    for t in range(NT):
        E("g", lambda e, t=t: e.dma_start(
            out=Xsb[:, t * 4096:(t + 1) * 4096],
            in_=xsh[t * 128:(t + 1) * 128, :]).then_inc(SM["sxl"], 16))
        inc("sxl", 16)
    for dst, src in [(cos_sb, cosc), (sin_sb, sinc), (id_sb, identc),
                     (atri_sb, atric), (mmask_sb, mmaskc), (ones_sb, onesc),
                     (esb, esbc), (rb_sb, rbias), (pcol_sb, pcolv),
                     (emap_i, emap), (cims, cimsc)]:
        E("g", lambda e, d=dst, s=src: e.dma_start(out=d[:, :], in_=s)
          .then_inc(SM["scl"], 16))
        inc("scl", 16)
    W("s", "scl", cv("scl"))
    E("s", lambda e: e.copy(emap_f[:, :], emap_i[:, :]).then_inc(SM["SE"], 1))
    inc("SE")
    W("v", "SE", cv("SE"))
    for ei in range(8):
        E("v", lambda e, ei=ei: e.tensor_scalar(
            msb[:, ei:ei + 1], emap_f[:, :], cims[0:15, 5 + ei:6 + ei], None,
            ALU.is_equal).then_inc(SM["VE"], 1))
        inc("VE")

    x_ready_ve = [0] * NT
    xlnT_ve = [0] * NT
    xln_free_pe = [0]

    def emit_layernorm(t, first):
        xt = Xsb[:, t * 4096:(t + 1) * 4096]
        su, mu, ms_, vr, sd, rs, sq = (colz[:, 0:1], colz[:, 1:2], colz[:, 2:3],
                                       colz[:, 3:4], colz[:, 4:5], colz[:, 5:6],
                                       colz[:, 6:7])
        if first:
            W("v", "sxl", 16 * NT)
            W("s", "sxl", 16 * NT)
        else:
            W("v", "VE", x_ready_ve[t])
            W("s", "VE", x_ready_ve[t])
        E("v", lambda e, xt=xt: e.reduce_sum(su, xt, axis=AX)
          .then_inc(SM["VE"], 1))
        inc("VE")
        for sl in range(8):
            E("s", lambda e, xt=xt, sl=sl: e.activation(
                junkb[:, :], xt[:, sl * 512:(sl + 1) * 512], AF.Square,
                accum_out=colz[:, 8 + sl:9 + sl])
              .then_inc(SM["SE"], 1) if sl == 7 else
              e.activation(junkb[:, :], xt[:, sl * 512:(sl + 1) * 512],
                           AF.Square, accum_out=colz[:, 8 + sl:9 + sl]))
        inc("SE")
        W("v", "SE", cv("SE"))
        E("v", lambda e: e.reduce_sum(sq, colz[:, 8:16], axis=AX)
          .then_inc(SM["VE"], 1))
        W("v", "VE", inc("VE"))
        E("v", lambda e: e.tensor_scalar(mu, su, cims[:, 0:1], None, ALU.mult))
        E("v", lambda e: e.tensor_scalar(ms_, sq, cims[:, 0:1], None, ALU.mult))
        E("v", lambda e: e.scalar_tensor_tensor(vr, mu, mu[:, 0:1], ms_,
                                                ALU.mult, ALU.subtract)
          .then_inc(SM["VE"], 1))
        inc("VE")
        W("s", "VE", cv("VE"))
        E("s", lambda e: e.activation(sd, vr, AF.Sqrt, bias=f32(1e-5),
                                      scale=f32(-1.0)).then_inc(SM["SE"], 1))
        inc("SE")
        W("v", "SE", cv("SE"))
        W("v", "PE", xln_free_pe[0])
        E("v", lambda e: e.reciprocal(rs, sd))
        E("v", lambda e, xt=xt: e.tensor_scalar(xln_bf[:, :], xt, mu[:, 0:1],
                                                rs[:, 0:1], ALU.subtract,
                                                ALU.mult).then_inc(SM["VE"], 1))
        ve_ln = inc("VE")
        W("t", "VE", ve_ln)
        for gi in range(4):
            bank, bn = (psT, "psT") if gi % 2 == 0 else (psU, "psU")
            twb(bn)
            for kb in range(8):
                col = gi * 8 + kb
                mm(bank[:, kb * 128:(kb + 1) * 128],
                   xln_bf[:, col * 128:(col + 1) * 128], id_sb[:, :],
                   True, True, tr=True, pe=(kb == 7))
            W("v", "PE", cv("PE"))
            dst = xlnT[:, t * 4096 + gi * 1024: t * 4096 + (gi + 1) * 1024]
            E("v", lambda e, d=dst, b=bank: e.tensor_copy(d, b[:, :])
              .then_inc(SM["VE"], 1))
            bank_free[bn] = ("VE", inc("VE"))
        xln_free_pe[0] = cv("PE")
        xlnT_ve[t] = cv("VE")

    stg_cnt = {"q": [0, 0], "k": [0, 0], "v": [0, 0]}

    def emit_qkv(L):
        robuf_free = [0, 0]
        W("y", "CC", cv("CC"))
        for n in ("szs0", "szs1", "szl0", "szl1"):
            W("v", n, cv(n))
        def wqchunk(bi, c0, c1):
            if bi < 2:
                return wb[bi][:, c0:c1]
            return arena[:, 32768 + c0:32768 + c1]

        freed = [cv("PE")] * 3
        for si, s in enumerate([8, 9, 10, 11, 0, 1, 2, 3, 4, 5, 6, 7]):
            if s == 0:
                emit_ag()
            rb0, rb1 = (2 * si) % 3, (2 * si + 1) % 3
            W("g", "PE", freed[rb0])
            E("g", lambda e, a=qkvw[L, s, :, 0:16, :], b=wqchunk(rb0, 0, 8192),
              sm=SM["sw%d" % rb0]: e.dma_start(out=b, in_=a)
              .then_inc(sm, 16))
            inc("sw%d" % rb0, 16)
            W("g", "PE", freed[rb1])
            E("g", lambda e, a=qkvw[L, s, :, 16:32, :], b=wqchunk(rb1, 0, 8192),
              sm=SM["sw%d" % rb1]: e.dma_start(out=b, in_=a)
              .then_inc(sm, 16))
            inc("sw%d" % rb1, 16)
            W("t", "sw%d" % rb0, cv("sw%d" % rb0))
            W("t", "sw%d" % rb1, cv("sw%d" % rb1))
            pe_last = 0
            for t in range(NT):
                bank, bn = (psA, "psA") if t % 2 == 0 else (psB, "psB")
                twb(bn)
                W("t", "VE", xlnT_ve[t])
                for kt in range(32):
                    mm(bank[:, :],
                       xlnT[:, t * 4096 + kt * 128: t * 4096 + (kt + 1) * 128],
                       wqchunk(rb0 if kt < 16 else rb1, (kt % 16) * 512,
                               ((kt % 16) + 1) * 512),
                       start=(kt == 0), stop=(kt == 31),
                       pe=(kt == 31) or (t == NT - 1 and kt == 15))
                    if t == NT - 1 and kt == 15:
                        freed[rb0] = cv("PE")
                pe_last = cv("PE")
                if s < 10:
                    W("v", "PE", pe_last)
                    rp = t % 2
                    W("v", "PE", robuf_free[rp])
                    ct = cos_sb[:, t * 64:(t + 1) * 64]
                    st_ = sin_sb[:, t * 64:(t + 1) * 64]
                    for i in range(4):
                        b0 = i * 128
                        E("v", lambda e, b0=b0, bank=bank, ct=ct:
                          e.tensor_mul(t1f[:, 0:64], bank[:, b0:b0 + 64], ct))
                        E("v", lambda e, b0=b0, bank=bank, ct=ct:
                          e.tensor_mul(t1f[:, 64:128],
                                       bank[:, b0 + 64:b0 + 128], ct))
                        E("v", lambda e, b0=b0, bank=bank, st_=st_:
                          e.tensor_mul(t2f[:, 0:64],
                                       bank[:, b0 + 64:b0 + 128], st_))
                        E("v", lambda e, b0=b0, bank=bank, st_=st_:
                          e.tensor_mul(t2f[:, 64:128], bank[:, b0:b0 + 64], st_))
                        E("v", lambda e, b0=b0, rp=rp:
                          e.tensor_sub(robuf[rp][:, b0:b0 + 64],
                                       t1f[:, 0:64], t2f[:, 0:64]))
                        if i == 3:
                            E("v", lambda e, b0=b0, rp=rp:
                              e.tensor_add(robuf[rp][:, b0 + 64:b0 + 128],
                                           t1f[:, 64:128], t2f[:, 64:128])
                              .then_inc(SM["VE"], 1))
                        else:
                            E("v", lambda e, b0=b0, rp=rp:
                              e.tensor_add(robuf[rp][:, b0 + 64:b0 + 128],
                                           t1f[:, 64:128], t2f[:, 64:128]))
                    bank_free[bn] = ("VE", inc("VE"))
                    W("t", "VE", cv("VE"))
                    bank2, bn2 = (psT, "psT") if t % 2 == 0 else (psU, "psU")
                    twb(bn2)
                    for i in range(4):
                        mm(bank2[:, i * 128:(i + 1) * 128],
                           robuf[rp][:, i * 128:(i + 1) * 128], id_sb[:, :],
                           True, True, tr=True, pe=(i == 3))
                    robuf_free[rp] = cv("PE")
                    kind = "q" if s < 8 else "k"
                    stg = qstage if kind == "q" else kstage
                    sn = ("sqs%d" if kind == "q" else "sks%d")
                    c = stg_cnt[kind]
                    p = (c[0] + c[1]) % 2
                    W("v", sn % p, 16 * c[p])
                    W("v", "PE", cv("PE"))
                    E("v", lambda e, p=p, b=bank2, stg=stg:
                      e.tensor_copy(stg[p][:, :], b[:, 0:512])
                      .then_inc(SM["VE"], 1))
                    bank_free[bn2] = ("VE", inc("VE"))
                    W("y", "VE", cv("VE"))
                    if kind == "q":
                        dst = qtp[t, 4 * s:4 * s + 4].transpose([1, 0, 2])
                    else:
                        g0 = 4 * (s - 8)
                        dst = agi_kt[g0:g0 + 4, :, t, :].transpose([1, 0, 2])
                    E("y", lambda e, d=dst, p=p, stg=stg, sm=SM[sn % p]:
                      e.dma_start(out=d, in_=stg[p][:, :]).then_inc(sm, 16))
                    c[p] += 1
                    inc(sn % p, 16)
                else:
                    c = stg_cnt["v"]
                    p = (c[0] + c[1]) % 2
                    W("v", "svs%d" % p, 16 * c[p])
                    W("v", "PE", pe_last)
                    E("v", lambda e, p=p, bank=bank:
                      e.tensor_copy(vstage[p][:, :], bank[:, :])
                      .then_inc(SM["VE"], 1))
                    bank_free[bn] = ("VE", inc("VE"))
                    W("y", "VE", cv("VE"))
                    dst = agi_v[t, :, (s - 10) * 512:(s - 10) * 512 + 512]
                    E("y", lambda e, d=dst, p=p, sm=SM["svs%d" % p]:
                      e.dma_start(out=d, in_=vstage[p][:, :]).then_inc(sm, 16))
                    c[p] += 1
                    inc("svs%d" % p, 16)
            freed[rb1] = pe_last
        wb_free_pe[0] = freed[0]
        wb_free_pe[1] = freed[1]

    def emit_ag():
        for n in ["sks0", "sks1", "svs0", "svs1", "skv0", "skv1"]:
            W("g", n, cv(n))
        E("g", lambda e: e.collective_compute(
            "AllGather", ALU.bypass,
            replica_groups=[[0, 1, 2, 3], [4, 5, 6, 7]],
            ins=[ag_in.opt()], outs=[ag_out_t.ap().opt()])
          .then_inc(SM["CC"], 1))
        inc("CC")

    last_z_ve = [0, 0, 0, 0]
    kv_free = [0, 0]
    qt_free = [0, 0]

    def emit_attention(L, grp):
        W("y", "sqs0", cv("sqs0"))
        W("y", "sqs1", cv("sqs1"))
        for t in range(4 * grp, 4 * grp + 4):
            tm = t % 4
            nch = t + 1
            for g in range(8):
                gp = g % 2
                W("y", "CC", cv("CC"))
                W("y", "PE", kv_free[gp])
                for kc in range(nch):
                    src = ago_kt[:, g, :, kc, :].transpose([1, 0, 2])
                    E("y", lambda e, s_=src, gp=gp, kc=kc, sm=SM["skv%d" % gp]:
                      e.dma_start(out=ktg[gp][:, kc * 512:(kc + 1) * 512], in_=s_)
                      .then_inc(sm, 16))
                    inc("skv%d" % gp, 16)
                    srcv = ago_v[:, kc, :, g * 128:(g + 1) * 128].transpose([1, 0, 2])
                    E("y", lambda e, s_=srcv, gp=gp, kc=kc, sm=SM["skv%d" % gp]:
                      e.dma_start(out=vg[gp][:, kc * 512:(kc + 1) * 512], in_=s_)
                      .then_inc(sm, 16))
                    inc("skv%d" % gp, 16)
                W("y", "PE", qt_free[gp])
                E("y", lambda e, t=t, g=g, gp=gp, sm=SM["sqt%d" % gp]:
                  e.dma_start(out=qtq[gp][:, :],
                              in_=qtp[t, 4 * g:4 * g + 4].transpose([1, 0, 2]))
                  .then_inc(sm, 16))
                inc("sqt%d" % gp, 16)

                se_exp = [[0] * nch for _ in range(4)]

                def emit_scores(i):
                    if i == 0:
                        W("t", "skv%d" % gp, cv("skv%d" % gp))
                        W("t", "sqt%d" % gp, cv("sqt%d" % gp))
                    W("s", "VE", last_z_ve[i])
                    for kc in range(nch):
                        bn = "psS%d" % (kc % 2)
                        bank = psS[kc % 2]
                        twb(bn)
                        diag = (kc == t)
                        mm(bank[:, :], qtq[gp][:, i * 128:(i + 1) * 128],
                           ktg[gp][:, kc * 512:(kc + 1) * 512],
                           start=True, stop=not diag, pe=not diag)
                        if diag:
                            mm(bank[:, :], atri_sb[:, :], mmask_sb[:, :],
                               start=False, stop=True, pe=True)
                        W("s", "PE", cv("PE"))
                        E("s", lambda e, i=i, kc=kc, bank=bank:
                          e.activation(Pst[i % 2][:, kc * 512:(kc + 1) * 512],
                                       bank[:, :], AF.Exp, bias=EBIAS,
                                       scale=SCALE,
                                       accum_out=zc[:, i * 8 + kc:i * 8 + kc + 1])
                          .then_inc(SM["SE"], 1))
                        se_exp[i][kc] = inc("SE")
                        bank_free[bn] = ("SE", se_exp[i][kc])
                    W("v", "SE", se_exp[i][nch - 1])
                    E("v", lambda e, i=i, nch=nch: e.reduce_sum(
                        colz[:, 8:9], zc[:, i * 8:i * 8 + nch], axis=AX))
                    E("v", lambda e: e.tensor_copy(colz[:, 2:3], colz[:, 2:3]))
                    E("v", lambda e: e.tensor_copy(colz[:, 2:3], colz[:, 2:3]))
                    E("v", lambda e, i=i: e.reciprocal(rinvq[:, i:i + 1],
                                                       colz[:, 8:9])
                      .then_inc(SM["VE"], 1))
                    last_z_ve[i] = inc("VE")

                def emit_ptav(i, first_av, last_av):
                    nblk = 4 * nch
                    for kb0 in range(0, nblk, 8):
                        kbn = min(8, nblk - kb0)
                        bank2, bn2 = ((psT, "psT") if (kb0 // 8) % 2 == 0
                                      else (psU, "psU"))
                        twb(bn2)
                        W("t", "SE", se_exp[i][min(nch - 1, (kb0 + kbn - 1) // 4)])
                        for kk in range(kbn):
                            kb = kb0 + kk
                            mm(bank2[:, kk * 128:(kk + 1) * 128],
                               Pst[i % 2][:, kb * 128:(kb + 1) * 128],
                               id_sb[:, :], True, True, tr=True,
                               pe=(kk == kbn - 1))
                        W("v", "PE", cv("PE"))
                        E("v", lambda e, kb0=kb0, kbn=kbn, b=bank2:
                          e.tensor_copy(pt_sb[:, kb0 * 128:(kb0 + kbn) * 128],
                                        b[:, 0:kbn * 128]).then_inc(SM["VE"], 1))
                        bank_free[bn2] = ("VE", inc("VE"))
                    W("t", "VE", cv("VE"))
                    if first_av:
                        eng, c = bank_free["psO"]
                        W("t", eng, c)
                    for kb in range(nblk):
                        mm(psO[:, i * 128:(i + 1) * 128],
                           pt_sb[:, kb * 128:(kb + 1) * 128],
                           vg[gp][:, kb * 128:(kb + 1) * 128],
                           start=(first_av and kb == 0),
                           stop=(last_av and kb == nblk - 1),
                           pe=(kb == nblk - 1))

                emit_scores(0)
                for i in range(1, 4):
                    emit_scores(i)
                    emit_ptav(i - 1, first_av=(i == 1), last_av=False)
                emit_ptav(3, first_av=False, last_av=True)
                qt_free[gp] = cv("PE")
                kv_free[gp] = cv("PE")
                W("v", "PE", cv("PE"))
                for i in range(4):
                    last = (i == 3)
                    if last:
                        E("v", lambda e, i=i: e.tensor_scalar(
                            ostage[:, i * 128:(i + 1) * 128],
                            psO[:, i * 128:(i + 1) * 128],
                            rinvq[:, i:i + 1], None, ALU.mult)
                          .then_inc(SM["VE"], 1))
                    else:
                        E("v", lambda e, i=i: e.tensor_scalar(
                            ostage[:, i * 128:(i + 1) * 128],
                            psO[:, i * 128:(i + 1) * 128],
                            rinvq[:, i:i + 1], None, ALU.mult))
                bank_free["psO"] = ("VE", inc("VE"))
                W("t", "VE", cv("VE"))
                twb("psT")
                for i in range(4):
                    mm(psT[:, i * 128:(i + 1) * 128],
                       ostage[:, i * 128:(i + 1) * 128], id_sb[:, :],
                       True, True, tr=True, pe=(i == 3))
                W("v", "PE", cv("PE"))
                E("v", lambda e, tm=tm, g=g: e.tensor_copy(
                    ot4[:, tm * 4096 + g * 512: tm * 4096 + (g + 1) * 512],
                    psT[:, 0:512]).then_inc(SM["VE"], 1))
                bank_free["psT"] = ("VE", inc("VE"))

    def emit_wo(L, grp):
        for s in range(8):
            load_wstrip(wow[L, s, :, 0:16, :], 0)
            load_wstrip(wow[L, s, :, 16:32, :], 1)
            W("t", "sw0", cv("sw0"))
            W("t", "sw1", cv("sw1"))
            W("t", "VE", cv("VE"))
            pe_last = 0
            for t in range(4 * grp, 4 * grp + 4):
                tm = t % 4
                bank, bn = (psA, "psA") if t % 2 == 0 else (psB, "psB")
                twb(bn)
                for kb in range(32):
                    mm(bank[:, :],
                       ot4[:, tm * 4096 + kb * 128: tm * 4096 + (kb + 1) * 128],
                       wb[kb // 16][:, (kb % 16) * 512:((kb % 16) + 1) * 512],
                       start=(kb == 0), stop=(kb == 31), pe=(kb == 31))
                pe_last = cv("PE")
                W("v", "PE", pe_last)
                xs = Xsb[:, t * 4096 + s * 512: t * 4096 + (s + 1) * 512]
                E("v", lambda e, xs=xs, bank=bank:
                  e.tensor_add(xs, xs, bank[:, :]).then_inc(SM["VE"], 1))
                bank_free[bn] = ("VE", inc("VE"))
                if s == 7:
                    x_ready_ve[t] = cv("VE")
            wb_free_pe[0] = pe_last
            wb_free_pe[1] = pe_last

    def emit_ut(t, scale_pcol):
        twb("psX")
        W("t", "VE", xlnT_ve[t])
        W("t", "scl2", cv("scl2"))
        for kt in range(32):
            mm(psX[:, 0:128],
               xlnT[:, t * 4096 + kt * 128: t * 4096 + (kt + 1) * 128],
               acat_sb[:, kt * 128:(kt + 1) * 128],
               start=(kt == 0), stop=(kt == 31), pe=(kt == 31))
        W("v", "PE", cv("PE"))
        E("v", lambda e: e.tensor_copy(ustage[:, :], psX[:, 0:128])
          .then_inc(SM["VE"], 1))
        bank_free["psX"] = ("VE", inc("VE"))
        W("t", "VE", cv("VE"))
        twb("psT")
        mm(psT[:, 0:128], ustage[:, :], id_sb[:, :], True, True,
           tr=True, pe=True)
        W("v", "PE", cv("PE"))
        if scale_pcol:
            E("v", lambda e, t=t: e.tensor_scalar(
                ut_sb[:, t * 128:(t + 1) * 128], psT[:, 0:128],
                pcol_sb[:, 0:1], None, ALU.mult).then_inc(SM["VE"], 1))
        else:
            E("v", lambda e: e.tensor_copy(htmp[:, :], psT[:, 0:128])
              .then_inc(SM["VE"], 1))
        bank_free["psT"] = ("VE", inc("VE"))

    def emit_ffn0():
        for t in range(NT):
            emit_layernorm(t, first=False)
            emit_ut(t, scale_pcol=True)
        zs_cnt = [0, 0]
        W("s", "svs0", cv("svs0"))
        W("s", "svs1", cv("svs1"))
        W("y", "sks0", cv("sks0"))
        W("y", "sks1", cv("sks1"))
        for s in range(8):
            load_wstrip(ffnw[0, s, :, 0:16, :], 0)
            load_wstrip(ffnw[0, s, :, 16:32, :], 1)
            load_bias_slice(0, s)
            W("t", "sw0", cv("sw0"))
            W("t", "sw1", cv("sw1"))
            W("t", "scl2", cv("scl2"))
            pe_last = 0
            for t in range(NT):
                bank, bn = (psA, "psA") if t % 2 == 0 else (psB, "psB")
                twb(bn)
                for kt in range(32):
                    mm(bank[:, :],
                       xlnT[:, t * 4096 + kt * 128: t * 4096 + (kt + 1) * 128],
                       wb[kt // 16][:, (kt % 16) * 512:((kt % 16) + 1) * 512],
                       start=(kt == 0), stop=False)
                mm(bank[:, :], ones_sb[:, :], bias_sb[:, :],
                   start=False, stop=True, pe=True)
                pe_last = cv("PE")
                bias_free_pe[0] = pe_last
                p = (zs_cnt[0] + zs_cnt[1]) % 2
                W("s", "szs%d" % p, 16 * zs_cnt[p])
                W("s", "PE", pe_last)
                E("s", lambda e, p=p, bank=bank:
                  e.copy(zstage[p][:, :], bank[:, :]).then_inc(SM["SE"], 1))
                se_z = inc("SE")
                E("s", lambda e, t=t, s=s, bank=bank: e.activation(
                    junkb[:, :], bank[:, :], AF.Square,
                    accum_out=szc[:, t * 8 + s: t * 8 + s + 1])
                  .then_inc(SM["SE"], 1))
                bank_free[bn] = ("SE", inc("SE"))
                W("y", "SE", se_z)
                E("y", lambda e, t=t, s=s, p=p, sm=SM["szs%d" % p]:
                  e.dma_start(out=zpark[t, :, s * 512:(s + 1) * 512],
                              in_=zstage[p][:, :]).then_inc(sm, 16))
                zs_cnt[p] += 1
                inc("szs%d" % p, 16)
            wb_free_pe[0] = pe_last
            wb_free_pe[1] = pe_last
        W("t", "VE", cv("VE"))
        for t in range(NT):
            for s in range(8):
                bank, bn = (psA, "psA") if s % 2 == 0 else (psB, "psB")
                twb(bn)
                mm(bank[:, :], ut_sb[:, t * 128:(t + 1) * 128],
                   bcat_sb[:, s * 512:(s + 1) * 512], True, True, pe=True)
                W("s", "PE", cv("PE"))
                E("s", lambda e, t=t, s=s, bank=bank: e.activation(
                    junkb[:, :], bank[:, :], AF.Square,
                    accum_out=smc[:, t * 8 + s: t * 8 + s + 1])
                  .then_inc(SM["SE"], 1))
                bank_free[bn] = ("SE", inc("SE"))
        W("v", "SE", cv("SE"))
        for t in range(NT):
            E("v", lambda e, t=t: e.reduce_sum(zc[:, t:t + 1],
                                               szc[:, t * 8:t * 8 + 8], axis=AX))
        for t in range(NT):
            E("v", lambda e, t=t: e.reduce_sum(zc[:, 8 + t:9 + t],
                                               smc[:, t * 8:t * 8 + 8], axis=AX))
        E("v", lambda e: e.tensor_copy(colz[:, 2:3], colz[:, 2:3])
          .then_inc(SM["VE"], 1))
        inc("VE")
        W("s", "VE", cv("VE"))
        E("s", lambda e: e.activation(zc[:, 16:24], zc[:, 0:8], AF.Sqrt))
        E("s", lambda e: e.activation(zc[:, 24:32], zc[:, 8:16], AF.Sqrt)
          .then_inc(SM["SE"], 1))
        inc("SE")
        W("v", "SE", cv("SE"))
        E("v", lambda e: e.tensor_scalar(szc[:, 0:8], zc[:, 24:32],
                                         cims[:, 3:4], None, ALU.add))
        E("v", lambda e: e.tensor_copy(colz[:, 2:3], colz[:, 2:3]))
        E("v", lambda e: e.tensor_copy(colz[:, 2:3], colz[:, 2:3]))
        E("v", lambda e: e.reciprocal(szc[:, 8:16], szc[:, 0:8]))
        E("v", lambda e: e.tensor_copy(colz[:, 2:3], colz[:, 2:3]))
        E("v", lambda e: e.tensor_copy(colz[:, 2:3], colz[:, 2:3]))
        E("v", lambda e: e.tensor_mul(szc[:, 16:24], zc[:, 16:24],
                                      szc[:, 8:16]))
        E("v", lambda e: e.tensor_copy(colz[:, 2:3], colz[:, 2:3]))
        E("v", lambda e: e.tensor_copy(colz[:, 2:3], colz[:, 2:3]))
        E("v", lambda e: e.tensor_scalar(gam[:, 0:8], szc[:, 16:24],
                                         cims[:, 1:2], cims[:, 2:3],
                                         ALU.mult, ALU.min))
        E("v", lambda e: e.tensor_copy(colz[:, 2:3], colz[:, 2:3]))
        E("v", lambda e: e.tensor_copy(colz[:, 2:3], colz[:, 2:3])
          .then_inc(SM["VE"], 1))
        inc("VE")
        ve_gam = cv("VE")
        W("y", "szs0", cv("szs0"))
        W("y", "szs1", cv("szs1"))
        zl_cnt = [0, 0]
        zld_free = [ve_gam, ve_gam]
        for t in range(NT):
            for s in range(8):
                p = (zl_cnt[0] + zl_cnt[1]) % 2
                W("y", "VE", zld_free[p])
                E("y", lambda e, t=t, s=s, p=p, sm=SM["szl%d" % p]:
                  e.dma_start(out=zld[p][:, :],
                              in_=zpark[t, :, s * 512:(s + 1) * 512])
                  .then_inc(sm, 16))
                zl_cnt[p] += 1
                inc("szl%d" % p, 16)
                bank, bn = (psA, "psA") if s % 2 == 0 else (psB, "psB")
                twb(bn)
                mm(bank[:, :], ut_sb[:, t * 128:(t + 1) * 128],
                   bcat_sb[:, s * 512:(s + 1) * 512], True, True, pe=True)
                W("v", "PE", cv("PE"))
                W("v", "VE", ve_gam)
                W("v", "szl%d" % p, cv("szl%d" % p))
                xs = Xsb[:, t * 4096 + s * 512: t * 4096 + (s + 1) * 512]
                E("v", lambda e, xs=xs, p=p: e.tensor_add(xs, xs, zld[p][:, :]))
                E("v", lambda e, t=t, xs=xs, bank=bank:
                  e.scalar_tensor_tensor(xs, bank[:, :], gam[:, t:t + 1],
                                         xs, ALU.mult, ALU.add)
                  .then_inc(SM["VE"], 1))
                bank_free[bn] = ("VE", inc("VE"))
                zld_free[p] = cv("VE")
                if s == 7:
                    x_ready_ve[t] = cv("VE")

    def emit_ffn1():
        pr_free_ve = 0
        for t in range(NT):
            emit_layernorm(t, first=False)
            emit_ut(t, scale_pcol=False)
            twb("psX")
            for kt in range(32):
                mm(psX[:, 0:15],
                   xlnT[:, t * 4096 + kt * 128: t * 4096 + (kt + 1) * 128],
                   rwt_sb[:, kt * 15:(kt + 1) * 15],
                   start=(kt == 0), stop=False)
            mm(psX[:, 0:15], ones_sb[:, :], rb_sb[:, :], start=False,
               stop=True, pe=True)
            W("s", "PE", cv("PE"))
            W("s", "VE", pr_free_ve)
            E("s", lambda e: e.activation(pr_sb[:, 0:15], psX[:, 0:15], AF.Exp,
                                          accum_out=rcol[:, 0:1])
              .then_inc(SM["SE"], 1))
            bank_free["psX"] = ("SE", inc("SE"))
            W("v", "SE", cv("SE"))
            E("v", lambda e: e.reciprocal(rcol[:, 1:2], rcol[:, 0:1]))
            E("v", lambda e: e.memset(rnk[:, 16:17], 0.0))
            E("v", lambda e: e.tensor_copy(colz[:, 2:3], colz[:, 2:3]))
            E("v", lambda e: e.tensor_scalar(pr_sb[:, 0:15], pr_sb[:, 0:15],
                                             rcol[:, 1:2], None, ALU.mult))
            E("v", lambda e: e.tensor_copy(colz[:, 2:3], colz[:, 2:3]))
            E("v", lambda e: e.tensor_copy(colz[:, 2:3], colz[:, 2:3]))
            for n in range(15):
                E("v", lambda e, n=n: e.tensor_scalar(
                    junk15[:, 0:15], pr_sb[:, 0:15], pr_sb[:, n:n + 1], None,
                    ALU.is_gt, ALU.add, accum_out=rnk[:, n:n + 1]))
            for n in range(1, 15):
                E("v", lambda e, n=n: e.tensor_scalar(
                    junk15[:, 0:n], pr_sb[:, 0:n], pr_sb[:, n:n + 1], None,
                    ALU.is_equal, ALU.add, accum_out=rnk[:, 16 + n:17 + n]))
            E("v", lambda e: e.tensor_copy(colz[:, 2:3], colz[:, 2:3]))
            E("v", lambda e: e.tensor_copy(colz[:, 2:3], colz[:, 2:3]))
            E("v", lambda e: e.tensor_add(rnk[:, 0:15], rnk[:, 0:15],
                                          rnk[:, 16:31]))
            E("v", lambda e: e.tensor_copy(colz[:, 2:3], colz[:, 2:3]))
            E("v", lambda e: e.tensor_copy(colz[:, 2:3], colz[:, 2:3]))
            E("v", lambda e: e.tensor_scalar(rnk[:, 16:31], rnk[:, 0:15],
                                             cims[:, 4:5], None, ALU.is_lt))
            E("v", lambda e: e.tensor_copy(colz[:, 2:3], colz[:, 2:3]))
            E("v", lambda e: e.tensor_copy(colz[:, 2:3], colz[:, 2:3]))
            E("v", lambda e: e.tensor_mul(psel[:, 0:15], pr_sb[:, 0:15],
                                          rnk[:, 16:31]))
            E("v", lambda e: e.memset(psel[:, 15:16], 0.0)
              .then_inc(SM["VE"], 1))
            pr_free_ve = inc("VE")
            W("t", "VE", cv("VE"))
            twb("psU")
            mm(psU[0:16, 0:128], psel[:, 0:16], id_sb[:, :], True, True,
               tr=True, pe=True)
            W("v", "PE", cv("PE"))
            E("v", lambda e: e.tensor_copy(pselT[:, :], psU[0:15, 0:128])
              .then_inc(SM["VE"], 1))
            bank_free["psU"] = ("VE", inc("VE"))
            W("t", "VE", cv("VE"))
            twb("psX")
            mm(psX[0:8, 0:128], msb[:, :], pselT[:, :], True, True, pe=True)
            W("v", "PE", cv("PE"))
            E("v", lambda e: e.tensor_copy(wt_bf[:, :], psX[0:8, 0:128])
              .then_inc(SM["VE"], 1))
            bank_free["psX"] = ("VE", inc("VE"))
            W("t", "VE", cv("VE"))
            twb("psX")
            mm(psX[:, 128:256], esb[:, :], wt_bf[:, :], True, True, pe=True)
            W("v", "PE", cv("PE"))
            E("v", lambda e: e.tensor_copy(wf_bf[:, :], psX[:, 128:256])
              .then_inc(SM["VE"], 1))
            bank_free["psX"] = ("VE", inc("VE"))
            E("v", lambda e: e.tensor_copy(colz[:, 2:3], colz[:, 2:3]))
            E("v", lambda e: e.tensor_copy(colz[:, 2:3], colz[:, 2:3]))
            E("v", lambda e, t=t: e.tensor_mul(ut_sb[:, t * 128:(t + 1) * 128],
                                               htmp[:, :], wf_bf[:, :])
              .then_inc(SM["VE"], 1))
            inc("VE")
        for s in range(8):
            load_wstrip(ffnw[1, s, :, 0:16, :], 0)
            load_wstrip(ffnw[1, s, :, 16:32, :], 1)
            load_bias_slice(1, s)
            W("t", "sw0", cv("sw0"))
            W("t", "sw1", cv("sw1"))
            W("t", "scl2", cv("scl2"))
            W("t", "VE", cv("VE"))
            pe_last = 0
            for t in range(NT):
                bank, bn = (psA, "psA") if t % 2 == 0 else (psB, "psB")
                twb(bn)
                for kt in range(32):
                    mm(bank[:, :],
                       xlnT[:, t * 4096 + kt * 128: t * 4096 + (kt + 1) * 128],
                       wb[kt // 16][:, (kt % 16) * 512:((kt % 16) + 1) * 512],
                       start=(kt == 0), stop=False)
                mm(bank[:, :], ones_sb[:, :], bias_sb[:, :],
                   start=False, stop=False)
                mm(bank[:, :], ut_sb[:, t * 128:(t + 1) * 128],
                   bcat_sb[:, s * 512:(s + 1) * 512], start=False, stop=True,
                   pe=True)
                pe_last = cv("PE")
                bias_free_pe[0] = pe_last
                W("v", "PE", pe_last)
                xs = Xsb[:, t * 4096 + s * 512: t * 4096 + (s + 1) * 512]
                E("v", lambda e, xs=xs, bank=bank:
                  e.tensor_add(xs, xs, bank[:, :]).then_inc(SM["VE"], 1))
                bank_free[bn] = ("VE", inc("VE"))
                if s == 7:
                    x_ready_ve[t] = cv("VE")
                    W("g", "VE", cv("VE"))
                    E("g", lambda e, t=t: e.dma_start(
                        out=y[t * 128:(t + 1) * 128, :],
                        in_=Xsb[:, t * 4096:(t + 1) * 4096])
                      .then_inc(SM["sys"], 16))
                    inc("sys", 16)
            wb_free_pe[0] = pe_last
            wb_free_pe[1] = pe_last

    def load_layer_consts(L):
        W("g", "PE", cv("PE"))
        W("g", "SE", cv("SE"))
        pairs = [(acat_sb, acat[L].rearrange("p k e -> p (k e)")),
                 (bcat_sb, bcat[L])]
        if L == 1:
            pairs.append((rwt_sb, rwt.rearrange("p k n -> p (k n)")))
        for dst, src in pairs:
            E("g", lambda e, d=dst, s=src: e.dma_start(out=d[:, :], in_=s)
              .then_inc(SM["scl2"], 16))
            inc("scl2", 16)

    for L in range(2):
        phase_barrier()
        for t in range(NT):
            emit_layernorm(t, first=(L == 0))
        W("t", "scl", cv("scl"))
        emit_qkv(L)
        phase_barrier()
        for grp in range(2):
            emit_attention(L, grp)
            emit_wo(L, grp)
        load_layer_consts(L)
        phase_barrier()
        if L == 0:
            emit_ffn0()
        else:
            emit_ffn1()
        phase_barrier()
    W("g", "sys", cv("sys"))

    with nc.Block() as blk:
        @blk.tensor
        def _(e):
            for f in prog["t"]:
                f(e)

        @blk.vector
        def _(e):
            for f in prog["v"]:
                f(e)

        @blk.scalar
        def _(e):
            for f in prog["s"]:
                f(e)

        @blk.gpsimd
        def _(e):
            for f in prog["g"]:
                f(e)

        @blk.sync
        def _(e):
            for f in prog["y"]:
                f(e)

    return nc


def _tile_w(w):
    Din, Dout = w.shape
    ns, kt = Dout // 512, Din // 128
    return np.ascontiguousarray(w.reshape(kt, 128, ns, 512).transpose(2, 1, 0, 3))


_NC_CACHE = [None]


def kernel(L0_attn_x, L0_attn_Wq, L0_attn_Wk, L0_attn_Wv, L0_attn_Wo,
           L0_ffn_W, L0_ffn_b, L0_ffn_A, L0_ffn_B, L0_ffn_p_scores,
           L1_attn_w_qkv, L1_attn_w_o, L1_ffn_base_W, L1_ffn_base_b,
           L1_ffn_router_W, L1_ffn_router_b, L1_ffn_A, L1_ffn_B,
           L1_ffn_expert_map):
    x = np.asarray(L0_attn_x, dtype=np.float32)

    qkv0 = np.concatenate([np.asarray(L0_attn_Wq, np.float32),
                           np.asarray(L0_attn_Wk, np.float32),
                           np.asarray(L0_attn_Wv, np.float32)], axis=1)
    wq1 = np.array(L1_attn_w_qkv, dtype=np.float32)
    perm = np.arange(128).reshape(64, 2).T.reshape(-1)
    for h in range(32):
        wq1[:, h * 128:(h + 1) * 128] = wq1[:, h * 128 + perm]
    for g in range(8):
        b0 = 4096 + g * 128
        wq1[:, b0:b0 + 128] = wq1[:, b0 + perm]
    qkvw = np.stack([_tile_w(qkv0), _tile_w(wq1)])
    wow = np.stack([_tile_w(np.asarray(L0_attn_Wo, np.float32)),
                    _tile_w(np.asarray(L1_attn_w_o, np.float32))])
    ffnw = np.stack([
        _tile_w(np.ascontiguousarray(np.asarray(L0_ffn_W, np.float32).T)),
        _tile_w(np.ascontiguousarray(np.asarray(L1_ffn_base_W, np.float32).T))])
    ac0 = np.asarray(L0_ffn_A, np.float32).transpose(1, 0, 2).reshape(4096, 128)
    ac1 = np.asarray(L1_ffn_A, np.float32).transpose(2, 0, 1).reshape(4096, 128)
    acat = np.stack([np.ascontiguousarray(a.reshape(32, 128, 128)
                                          .transpose(1, 0, 2))
                     for a in (ac0, ac1)])
    bcat = np.stack([np.asarray(L0_ffn_B, np.float32).reshape(128, 4096),
                     np.ascontiguousarray(np.asarray(L1_ffn_B, np.float32)
                                          .transpose(0, 2, 1)
                                          .reshape(128, 4096))])
    rwt = np.ascontiguousarray(np.asarray(L1_ffn_router_W, np.float32).T
                               .reshape(32, 128, 15).transpose(1, 0, 2))
    biasv = np.stack([np.asarray(L0_ffn_b, np.float32).reshape(1, 4096),
                      np.asarray(L1_ffn_base_b, np.float32).reshape(1, 4096)])
    rbias = np.asarray(L1_ffn_router_b, np.float32).reshape(1, 15)
    pcolv = np.repeat(np.asarray(L0_ffn_p_scores, np.float32), 16).reshape(128, 1)
    emap = np.asarray(L1_ffn_expert_map, np.int32).reshape(15, 1)

    identc = np.eye(128, dtype=np.float32).astype(BF)
    dd = np.arange(128)
    atric = (dd[:, None] >= dd[None, :]).astype(np.float32).astype(BF)
    onesc = np.ones((1, 128), np.float32).astype(BF)
    esbc = (2.0 * (np.arange(128)[None, :] // 16 == np.arange(8)[:, None])
            ).astype(np.float32).astype(BF)
    inv = (1.0 / (np.float32(10000.0) **
                  (np.arange(0, 128, 2, dtype=np.float32) / np.float32(128.0))
                  )).astype(np.float32)
    cims_h = np.ascontiguousarray(np.tile(np.array(
        [1.0 / 4096.0, 0.5, 1.0, 1e-6, 7.5, 0, 1, 2, 3, 4, 5, 6, 7],
        np.float32), (128, 1)))

    in_maps = []
    shard_blocks = []
    for core in range(8):
        b, j = divmod(core, 4)
        blocks = np.arange(j, 32, 4)
        shard_blocks.append((b, blocks))
        xs = np.ascontiguousarray(
            x[b].reshape(32, 128, 4096)[blocks].reshape(1024, 4096))
        pos = (blocks[:, None] * 128 + np.arange(128)[None, :]).reshape(-1)
        ang = pos[:, None].astype(np.float32) * inv[None, :]
        c64 = np.cos(ang).astype(np.float32)
        s64 = np.sin(ang).astype(np.float32)
        cosc = np.ascontiguousarray(
            c64.reshape(8, 128, 64).transpose(1, 0, 2).reshape(128, 512))
        sincv = np.ascontiguousarray(
            s64.reshape(8, 128, 64).transpose(1, 0, 2).reshape(128, 512))
        kr = np.arange(512)[None, :]
        mmaskc = (np.float32(NEG) * (kr >= dd[:, None] + j * 128 + 1)
                  ).astype(np.float32).astype(BF)
        in_maps.append({
            "xsh": xs, "qkvw": qkvw, "wow": wow, "ffnw": ffnw, "acat": acat,
            "bcat": bcat, "rwt": rwt, "biasv": biasv, "rbias": rbias,
            "pcolv": pcolv, "emap": emap, "identc": identc, "atric": atric,
            "onesc": onesc, "esbc": esbc, "cosc": cosc, "sinc": sincv,
            "mmaskc": mmaskc, "cimsc": cims_h,
        })

    if _NC_CACHE[0] is None:
        _NC_CACHE[0] = build_nc()
    nc = _NC_CACHE[0]
    res = run_bass_kernel_spmd(nc, in_maps, core_ids=list(range(8)))

    out = np.empty((2, 4096, 4096), dtype=np.float32)
    ov = out.reshape(2, 32, 128, 4096)
    for core in range(8):
        b, blocks = shard_blocks[core]
        yc = np.asarray(res.results[core]["y"]).reshape(8, 128, 4096)
        ov[b, blocks] = yc
    return out
